# version 01 (final)
# baseline (speedup 1.0000x reference)
import contextlib
import os
import numpy as np
import ml_dtypes
import concourse.bass as bass
import concourse.mybir as mybir
from concourse.bass_utils import run_bass_kernel_spmd

F32 = mybir.dt.float32
BF16 = mybir.dt.bfloat16
AF = mybir.ActivationFunctionType
ALU = mybir.AluOpType
AX = mybir.AxisListType.X

B, S, E, DI, Nst, DTR = 4, 4096, 1024, 2048, 16, 64
TC = 2048
NCORES = 8
bf16 = ml_dtypes.bfloat16


def cdiv(a, b):
    return (a + b - 1) // b


def nchunks(total, mx):
    out, o = [], 0
    while o < total:
        l = min(mx, total - o)
        out.append((o, l))
        o += l
    return out


class Prog:

    def __init__(self, nc):
        self.nc = nc
        self.ops = []

    def op(self, engine, fn, inc=None, fence=False):
        self.ops.append((engine, fn, inc, fence))

    def t(self, fn):
        self.op("tensor", fn)

    def v(self, fn):
        self.op("vector", fn)

    def s(self, fn):
        self.op("scalar", fn)

    def g(self, fn):
        self.op("gpsimd", fn)

    def dma(self, engine, fn, fence=False):
        self.op(engine, fn, inc=16, fence=fence)

    def emit(self, block, ch):
        total = [0]
        prev_eng = [None]
        i = 0
        n = len(self.ops)
        while i < n:
            j = i
            eng_name = self.ops[i][0]
            while j < n and self.ops[j][0] == eng_name:
                j += 1
            run = self.ops[i:j]
            need = total[0] if (prev_eng[0] is not None and i > 0) else 0
            wait_needed = prev_eng[0] is not None and prev_eng[0] != eng_name

            def section(eng, run=run, need=need, wait_needed=wait_needed):
                if wait_needed and need > 0:
                    eng.wait_ge(ch, need)
                for oi, (_, fn, inc, fence) in enumerate(run):
                    if fence and not (oi == 0 and wait_needed):
                        eng.wait_ge(ch, total[0])
                    r = fn(eng)
                    if r is None:
                        raise RuntimeError("op closure returned None")
                    pairs = r if isinstance(r, list) else [(r, inc or 1)]
                    for instr, k in pairs:
                        instr.then_inc(ch, k)
                        total[0] += k
            getattr(block, "sync" if eng_name == "sync" else eng_name)(section)
            prev_eng[0] = eng_name
            i = j
        self.ops = []
        return total[0]


def run_prog(nc, prog, name="p"):
    with nc.semaphore("ch_" + name) as ch, nc.Block(name) as blk:
        prog.emit(blk, ch)


TW = TC + 3
TSUB = 256
NSUB = TC // TSUB
WL = TSUB + 3


def build_launch_A():
    nc = bass.Bass("TRN2", target_bir_lowering=False)

    def din(name, shape, dt=F32):
        return nc.dram_tensor(name, shape, dt, kind="ExternalInput").ap()

    def dout(name, shape, dt=F32):
        return nc.dram_tensor(name, shape, dt, kind="ExternalOutput").ap()

    xw = din("xw", [E, TW])
    mask128 = din("mask128", [128, TW])
    g1in_p = din("g1in_p", [16, 2048])
    g2in_p = din("g2in_p", [16, 2048])
    g1xp_p = din("g1xp_p", [16, 256])
    g2xp_p = din("g2xp_p", [16, 768])
    g1dt_p = din("g1dt_p", [16, 256])
    g2dt_p = din("g2dt_p", [16, 512])
    in_b_r = din("in_b_r", [128, 32])
    convw_r = din("convw_r", [128, 64])
    convb_r = din("convb_r", [128, 16])
    xp_b_c = din("xp_b_c", [96, 1])
    dt_b_r = din("dt_b_r", [128, 16])
    ones_c = din("ones_c", [128, 1])
    ones_r_d = din("ones_r_d", [1, 128])

    dt_o = dout("dt_o", [DI, TC])
    u_o = dout("u_o", [DI, TC])
    b_o = dout("b_o", [Nst, TC])
    c_o = dout("c_o", [Nst, TC])

    scr = nc.dram_tensor("scr", [2048, 2048], BF16)

    with contextlib.ExitStack() as stack:
        sb = lambda name, shape, dt: stack.enter_context(nc.sbuf_tensor(name, shape, dt))
        psum = lambda name, shape: stack.enter_context(nc.psum_tensor(name, shape, F32))
        w_in = sb("w_in", [128, 8 * 4096], BF16)
        w_xp = sb("w_xp", [128, 16 * 96], BF16)
        w_dt = sb("w_dt", [64, 16 * 128], BF16)
        in_b_s = sb("in_b_s", [128, 32], F32)
        convw_s = sb("convw_s", [128, 64], F32)
        convb_s = sb("convb_s", [128, 16], F32)
        xp_b_s = sb("xp_b_s", [96, 1], F32)
        dt_b_s = sb("dt_b_s", [128, 16], F32)
        mask_s = sb("mask_s", [128, TW], F32)
        ones_f = sb("ones_f", [128, 1], F32)
        p = Prog(nc)
        for dst, src in ((in_b_s, in_b_r), (convw_s, convw_r), (convb_s, convb_r),
                         (xp_b_s, xp_b_c), (dt_b_s, dt_b_r), (mask_s, mask128),
                         (ones_f, ones_c)):
            p.dma("sync", lambda sy, dst=dst, src=src: sy.dma_start(out=dst[:], in_=src))

        _wmax = int(os.environ.get("K_WMAX", 99))
        specs = [] if _wmax < 1 else [
            ("in", g1in_p, g2in_p, 32, 64, 32, 64),
            ("xp", g1xp_p, g2xp_p, 32, 8, 64, 12),
            ("dt", g1dt_p, g2dt_p, 8, 32, 8, 64),
        ]
        wstack = contextlib.ExitStack()
        for name, g1p, g2p, si, sn, sj, sm in specs:
            M, Nn = si * sn, sj * sm
            g1s = wstack.enter_context(nc.sbuf_tensor(f"g1s_{name}", [16, M], BF16))
            g2s = wstack.enter_context(nc.sbuf_tensor(f"g2s_{name}", [16, Nn], BF16))
            outs = wstack.enter_context(nc.sbuf_tensor(f"outs_{name}", [128, cdiv(M, 128) * Nn], BF16))
            wps = wstack.enter_context(nc.psum_tensor(f"wps_{name}", [128, 512], F32))
            p.dma("gpsimd", lambda g, g1s=g1s, g1p=g1p, M=M: g.dma_start(out=g1s[:], in_=g1p[:, 0:M]))
            p.dma("gpsimd", lambda g, g2s=g2s, g2p=g2p, Nn=Nn: g.dma_start(out=g2s[:], in_=g2p[:, 0:Nn]))
            for mo, ml in (nchunks(M, 128) if _wmax >= 2 else []):
                for no, nl in nchunks(Nn, 512):
                    p.t(lambda t, wps=wps, g1s=g1s, g2s=g2s, mo=mo, ml=ml, no=no, nl=nl:
                        t.matmul(wps[0:ml, 0:nl], g1s[:, mo:mo + ml], g2s[:, no:no + nl],
                                 start=True, stop=True))
                    p.v(lambda v, outs=outs, wps=wps, mo=mo, ml=ml, no=no, nl=nl, Nn=Nn:
                        v.tensor_copy(outs[0:ml, (mo // 128) * Nn + no:(mo // 128) * Nn + no + nl],
                                      wps[0:ml, 0:nl]))
            for mo, ml in (nchunks(M, 128) if _wmax >= 3 else []):
                p.dma("gpsimd", lambda g, outs=outs, mo=mo, ml=ml, Nn=Nn:
                      g.dma_start(out=scr.ap()[mo:mo + ml, 0:Nn],
                                  in_=outs[0:ml, (mo // 128) * Nn:(mo // 128) * Nn + Nn]))
            _wspec = os.environ.get("K_WSPEC", "in,xp,dt").split(",")
            for ii in (range(si) if (_wmax >= 4 and name in _wspec) else []):
                off = (ii * sn) * 2048
                sap = bass.AP(scr, off, [[sm, sj], [2048, sn], [1, sm]])
                if name == "in":
                    kc, r0 = ii // 4, (ii % 4) * 32
                    dst = w_in[r0:r0 + sj, kc * 4096:(kc + 1) * 4096]
                elif name == "xp":
                    kc, r0 = ii // 2, (ii % 2) * 64
                    dst = w_xp[r0:r0 + sj, kc * 96:(kc + 1) * 96]
                else:
                    dst = w_dt[ii * sj:(ii + 1) * sj, 0:2048]
                p.dma("gpsimd", lambda g, sap=sap, dst=dst: g.dma_start(out=dst, in_=sap), fence=True)
        run_prog(nc, p, "wbuild")
        wstack.close()

        if True:
            xw_s = sb("xw_s", [128, 8 * WL], F32)
            xsq = sb("xsq", [128, WL], F32)
            xn_bf = sb("xn_bf", [128, 8 * WL], BF16)
            strow = sb("strow", [1, 6 * WL], F32)
            ones_r = sb("ones_r", [1, 128], F32)
            xcv = sb("xcv", [128, 16 * WL], F32)
            zg = sb("zg", [128, 16 * TSUB], F32)
            yb = sb("yb", [128, 16 * TSUB], F32)
            ybf = sb("ybf", [128, 16 * TSUB], BF16)
            dtu_bf = sb("dtu_bf", [64, TSUB], BF16)
            bc_s = sb("bc_s", [96, TSUB], F32)
            t1 = sb("t1", [128, TSUB], F32)
            t2 = sb("t2", [128, TSUB], F32)
            ps_a = psum("ps_a", [128, 512])
            ps_b = psum("ps_b", [128, 512])
            ps_st = psum("ps_st", [1, 1024])
            ps_r = psum("ps_r", [128, WL])
            ps_m = psum("ps_m", [128, WL])
            p = Prog(nc)
            p.dma("sync", lambda sy: sy.dma_start(out=ones_r[:, :], in_=ones_r_d))

            _nsub = int(os.environ.get("K_NSUB", NSUB))
            _pmax = int(os.environ.get("K_PMAX", 99))
            for sub in range(_nsub):
                c0 = sub * TSUB
                for kc in range(8):
                    p.dma("sync", lambda sy, kc=kc, c0=c0:
                          sy.dma_start(out=xw_s[:, kc * WL:(kc + 1) * WL],
                                       in_=xw[kc * 128:(kc + 1) * 128, c0:c0 + WL]))
                for kc in range(8):
                    p.t(lambda t, kc=kc: t.matmul(ps_st[0:1, 0:WL], ones_f[:, 0:1],
                                                  xw_s[:, kc * WL:(kc + 1) * WL],
                                                  start=(kc == 0), stop=(kc == 7)))
                for kc in range(8):
                    p.s(lambda s, kc=kc: s.activation(xsq[:, 0:WL],
                                                      xw_s[:, kc * WL:(kc + 1) * WL], AF.Square))
                    p.t(lambda t, kc=kc: t.matmul(ps_st[0:1, 512:512 + WL], ones_f[:, 0:1],
                                                  xsq[:, 0:WL], start=(kc == 0), stop=(kc == 7)))
                p.v(lambda v: v.tensor_scalar(strow[0:1, 0:WL], ps_st[0:1, 0:WL],
                                              1.0 / 1024.0, None, ALU.mult))
                p.v(lambda v: v.tensor_scalar(strow[0:1, WL:2 * WL], ps_st[0:1, 512:512 + WL],
                                              1.0 / 1024.0, None, ALU.mult))
                p.v(lambda v: v.tensor_mul(strow[0:1, 2 * WL:3 * WL], strow[0:1, 0:WL],
                                           strow[0:1, 0:WL]))
                p.v(lambda v: v.tensor_sub(strow[0:1, 2 * WL:3 * WL], strow[0:1, WL:2 * WL],
                                           strow[0:1, 2 * WL:3 * WL]))
                p.v(lambda v: v.tensor_scalar(strow[0:1, 2 * WL:3 * WL], strow[0:1, 2 * WL:3 * WL],
                                              1e-5, None, ALU.add))
                p.s(lambda s: s.activation(strow[0:1, 3 * WL:4 * WL], strow[0:1, 2 * WL:3 * WL],
                                           AF.Ln))
                p.s(lambda s: s.activation(strow[0:1, 3 * WL:4 * WL], strow[0:1, 3 * WL:4 * WL],
                                           AF.Exp, scale=-0.5))
                p.v(lambda v: v.tensor_mul(strow[0:1, 4 * WL:5 * WL], strow[0:1, 0:WL],
                                           strow[0:1, 3 * WL:4 * WL]))
                p.v(lambda v: v.tensor_scalar(strow[0:1, 4 * WL:5 * WL], strow[0:1, 4 * WL:5 * WL],
                                              -1.0, None, ALU.mult))
                p.t(lambda t: t.matmul(ps_r[:, 0:WL], ones_r[0:1, :],
                                       strow[0:1, 3 * WL:4 * WL], start=True, stop=True))
                p.t(lambda t: t.matmul(ps_m[:, 0:WL], ones_r[0:1, :],
                                       strow[0:1, 4 * WL:5 * WL], start=True, stop=True))
                for kc in range(8):
                    p.v(lambda v, kc=kc: v.tensor_mul(xw_s[:, kc * WL:(kc + 1) * WL],
                                                      xw_s[:, kc * WL:(kc + 1) * WL],
                                                      ps_r[:, 0:WL]))
                    p.v(lambda v, kc=kc: v.tensor_tensor(xn_bf[:, kc * WL:(kc + 1) * WL],
                                                         xw_s[:, kc * WL:(kc + 1) * WL],
                                                         ps_m[:, 0:WL], ALU.add))
                if _pmax < 2:
                    continue
                for oc in range(32):
                    wl = WL if oc < 16 else TSUB
                    coff = 0 if oc < 16 else 1
                    ps = ps_a if oc % 2 == 0 else ps_b
                    for kc in range(8):
                        p.t(lambda t, oc=oc, kc=kc, ps=ps, wl=wl, coff=coff:
                            t.matmul(ps[:, 0:wl],
                                     w_in[:, kc * 4096 + oc * 128: kc * 4096 + oc * 128 + 128],
                                     xn_bf[:, kc * WL + coff: kc * WL + coff + wl],
                                     start=(kc == 0), stop=(kc == 7)))
                    if oc < 16:
                        p.v(lambda v, oc=oc, ps=ps:
                            v.tensor_scalar(xcv[:, oc * WL:(oc + 1) * WL], ps[:, 0:WL],
                                            in_b_s[:, oc:oc + 1], None, ALU.add))
                        p.v(lambda v, oc=oc, c0=c0:
                            v.tensor_mul(xcv[:, oc * WL:(oc + 1) * WL],
                                         xcv[:, oc * WL:(oc + 1) * WL],
                                         mask_s[:, c0:c0 + WL]))
                    else:
                        zc = oc - 16
                        p.v(lambda v, zc=zc, ps=ps:
                            v.tensor_scalar(zg[:, zc * TSUB:(zc + 1) * TSUB], ps[:, 0:TSUB],
                                            in_b_s[:, 16 + zc:16 + zc + 1], None, ALU.add))
                if _pmax < 3:
                    continue
                for ccj in range(16):
                    xc = lambda k, ccj=ccj: xcv[:, ccj * WL + k: ccj * WL + k + TSUB]
                    p.v(lambda v, ccj=ccj, xc=xc:
                        v.tensor_scalar(yb[:, ccj * TSUB:(ccj + 1) * TSUB], xc(0),
                                        convw_s[:, ccj * 4:ccj * 4 + 1], None, ALU.mult))
                    for k in range(1, 4):
                        p.v(lambda v, ccj=ccj, k=k, xc=xc:
                            v.scalar_tensor_tensor(yb[:, ccj * TSUB:(ccj + 1) * TSUB], xc(k),
                                                   convw_s[:, ccj * 4 + k:ccj * 4 + k + 1],
                                                   yb[:, ccj * TSUB:(ccj + 1) * TSUB],
                                                   ALU.mult, ALU.add))
                    p.s(lambda s, ccj=ccj:
                        s.activation(yb[:, ccj * TSUB:(ccj + 1) * TSUB],
                                     yb[:, ccj * TSUB:(ccj + 1) * TSUB], AF.Silu,
                                     bias=convb_s[:, ccj:ccj + 1]))
                    p.v(lambda v, ccj=ccj:
                        v.tensor_mul(yb[:, ccj * TSUB:(ccj + 1) * TSUB],
                                     yb[:, ccj * TSUB:(ccj + 1) * TSUB],
                                     zg[:, ccj * TSUB:(ccj + 1) * TSUB]))
                    p.v(lambda v, ccj=ccj:
                        v.tensor_copy(ybf[:, ccj * TSUB:(ccj + 1) * TSUB],
                                      yb[:, ccj * TSUB:(ccj + 1) * TSUB]))
                    p.dma("sync", lambda sy, ccj=ccj, c0=c0:
                          sy.dma_start(out=u_o[ccj * 128:(ccj + 1) * 128, c0:c0 + TSUB],
                                       in_=yb[:, ccj * TSUB:(ccj + 1) * TSUB]))
                if _pmax < 4:
                    continue
                for kc in range(16):
                    p.t(lambda t, kc=kc:
                        t.matmul(ps_a[0:96, 0:TSUB], w_xp[:, kc * 96:(kc + 1) * 96],
                                 ybf[:, kc * TSUB:(kc + 1) * TSUB],
                                 start=(kc == 0), stop=(kc == 15)))
                p.v(lambda v: v.tensor_scalar(dtu_bf[0:64, :], ps_a[0:64, 0:TSUB],
                                              xp_b_s[0:64, 0:1], None, ALU.add))
                p.v(lambda v: v.tensor_scalar(bc_s[64:96, :], ps_a[64:96, 0:TSUB],
                                              xp_b_s[64:96, 0:1], None, ALU.add))
                p.dma("sync", lambda sy, c0=c0:
                      sy.dma_start(out=b_o[:, c0:c0 + TSUB], in_=bc_s[64:80, :]))
                p.dma("sync", lambda sy, c0=c0:
                      sy.dma_start(out=c_o[:, c0:c0 + TSUB], in_=bc_s[80:96, :]))
                if _pmax < 5:
                    continue
                for oc in range(16):
                    ps = ps_a if oc % 2 == 0 else ps_b
                    p.t(lambda t, oc=oc, ps=ps:
                        t.matmul(ps[:, 0:TSUB], w_dt[0:64, oc * 128:(oc + 1) * 128],
                                 dtu_bf[0:64, :], start=True, stop=True))
                    p.v(lambda v, oc=oc, ps=ps:
                        v.tensor_scalar(t1[:, :], ps[:, 0:TSUB],
                                        dt_b_s[:, oc:oc + 1], None, ALU.add))
                    p.s(lambda s: s.activation(t2[:, :], t1[:, :], AF.Abs))
                    p.s(lambda s: s.activation(t2[:, :], t2[:, :], AF.Exp, scale=-1.0))
                    p.s(lambda s: s.activation(t2[:, :], t2[:, :], AF.Ln, bias=1.0))
                    p.v(lambda v: v.tensor_scalar(t1[:, :], t1[:, :], 0.0, None, ALU.max))
                    p.v(lambda v: v.tensor_add(t1[:, :], t1[:, :], t2[:, :]))
                    p.dma("sync", lambda sy, oc=oc, c0=c0:
                          sy.dma_start(out=dt_o[oc * 128:(oc + 1) * 128, c0:c0 + TSUB],
                                       in_=t1[:, :]))
            run_prog(nc, p, "main")
    return nc


def host_prep_A(inp, core):
    b, h = core // 2, core % 2
    t0 = h * TC
    x = np.asarray(inp["L0_attn_x"][b])
    xT = np.ascontiguousarray(x.T)
    xw = np.zeros((E, TW), np.float32)
    mask = np.zeros((TW,), np.float32)
    lo, hi = t0 - 1, t0 + TC + 2
    slo, shi = max(lo, 0), min(hi, S)
    xw[:, slo - lo:shi - lo] = xT[:, slo:shi]
    mask[slo - lo:shi - lo] = 1.0
    mask128 = np.ascontiguousarray(np.broadcast_to(mask[None, :], (128, TW)))

    g1in = np.asarray(inp["L0_attn_in_g1"])
    g2in = np.asarray(inp["L0_attn_in_g2"])
    g1xp = np.asarray(inp["L0_attn_xp_g1"])
    g2xp = np.asarray(inp["L0_attn_xp_g2"])
    g1dt = np.asarray(inp["L0_attn_dt_g1"])
    g2dt = np.asarray(inp["L0_attn_dt_g2"])
    pr = lambda g: np.ascontiguousarray(g.transpose(2, 0, 1).reshape(16, -1))
    rs = lambda g: np.ascontiguousarray(g.reshape(16, -1))

    conv_w = np.asarray(inp["L0_attn_conv_w"])
    convw_r = np.ascontiguousarray(
        conv_w[:, 0, :].T.reshape(16, 128, 4).transpose(1, 0, 2).reshape(128, 64))
    col = lambda a, n: np.ascontiguousarray(np.asarray(a).reshape(n, 128).T)

    return {
        "xw": xw, "mask128": mask128,
        "g1in_p": pr(g1in), "g2in_p": rs(g2in),
        "g1xp_p": pr(g1xp), "g2xp_p": rs(g2xp),
        "g1dt_p": pr(g1dt), "g2dt_p": rs(g2dt),
        "in_b_r": col(inp["L0_attn_in_b"], 32),
        "convw_r": convw_r,
        "convb_r": col(inp["L0_attn_conv_b"], 16),
        "xp_b_c": np.ascontiguousarray(np.asarray(inp["L0_attn_xp_b"]).reshape(96, 1)),
        "dt_b_r": col(inp["L0_attn_dt_b"], 16),
        "ones_c": np.ones((128, 1), np.float32),
        "ones_r_d": np.ones((1, 128), np.float32),
    }


TQ = 1024
NQ = S // TQ


def build_launch_scan():
    nc = bass.Bass("TRN2", target_bir_lowering=False)

    def din(name, shape, dt=F32):
        return nc.dram_tensor(name, shape, dt, kind="ExternalInput").ap()

    dt_r = din("dt_r", [1024, S])
    u_r = din("u_r", [1024, S])
    b_r = din("b_r", [Nst, S])
    c_r = din("c_r", [Nst, S])
    a_log_r = din("a_log_r", [128, 128])
    ones_r_d = din("ones_r_d", [1, 128])
    y_o = nc.dram_tensor("y_o", [1024, S], F32, kind="ExternalOutput").ap()

    with contextlib.ExitStack() as stack:
        sb = lambda name, shape, dt: stack.enter_context(nc.sbuf_tensor(name, shape, dt))
        psum = lambda name, shape: stack.enter_context(nc.psum_tensor(name, shape, F32))
        a_s = sb("a_s", [128, 128], F32)
        ones_r = sb("ones_r", [1, 128], F32)
        brow = sb("brow", [1, TQ], F32)
        crow = sb("crow", [1, TQ], F32)
        b_rep = sb("b_rep", [128, 16 * TQ], F32)
        c_rep = sb("c_rep", [128, 16 * TQ], F32)
        dt_c = sb("dt_c", [128, TQ], F32)
        u_c = sb("u_c", [128, TQ], F32)
        du_c = sb("du_c", [128, TQ], F32)
        dA = sb("dA", [128, TQ + 2], F32)
        dBu = sb("dBu", [128, TQ + 2], F32)
        hbuf = sb("hbuf", [128, TQ + 2], F32)
        hC = sb("hC", [128, TQ], F32)
        yacc = sb("yacc", [128, TQ], F32)
        hlast = sb("hlast", [128, 128], F32)
        ps_b = psum("ps_b", [128, TQ])
        ps_c = psum("ps_c", [128, TQ])

        p = Prog(nc)
        p.dma("sync", lambda sy: sy.dma_start(out=ones_r[:, :], in_=ones_r_d))
        p.dma("sync", lambda sy: sy.dma_start(out=a_s[:, :], in_=a_log_r))
        p.s(lambda s: s.activation(a_s[:, :], a_s[:, :], AF.Exp))
        p.v(lambda v: v.tensor_scalar(a_s[:, :], a_s[:, :], -1.0, None, ALU.mult))
        p.v(lambda v: v.memset(dA[:, 0:2], 0.0))
        p.v(lambda v: v.memset(dBu[:, 0:1], 0.0))
        p.v(lambda v: v.memset(hlast[:, :], 0.0))

        for q in range(NQ):
            q0 = q * TQ
            for n in range(Nst):
                p.dma("sync", lambda sy, n=n, q0=q0:
                      sy.dma_start(out=brow[:, :], in_=b_r[n:n + 1, q0:q0 + TQ]))
                p.dma("sync", lambda sy, n=n, q0=q0:
                      sy.dma_start(out=crow[:, :], in_=c_r[n:n + 1, q0:q0 + TQ]))
                for no, nl in nchunks(TQ, 512):
                    p.t(lambda t, no=no, nl=nl: t.matmul(ps_b[:, no:no + nl], ones_r[0:1, :],
                                                         brow[0:1, no:no + nl], start=True, stop=True))
                    p.t(lambda t, no=no, nl=nl: t.matmul(ps_c[:, no:no + nl], ones_r[0:1, :],
                                                         crow[0:1, no:no + nl], start=True, stop=True))
                p.s(lambda s, n=n: s.copy(b_rep[:, n * TQ:(n + 1) * TQ], ps_b[:, :]))
                p.s(lambda s, n=n: s.copy(c_rep[:, n * TQ:(n + 1) * TQ], ps_c[:, :]))
            for dc in range(8):
                p.dma("sync", lambda sy, dc=dc, q0=q0:
                      sy.dma_start(out=dt_c[:, :], in_=dt_r[dc * 128:(dc + 1) * 128, q0:q0 + TQ]))
                p.dma("sync", lambda sy, dc=dc, q0=q0:
                      sy.dma_start(out=u_c[:, :], in_=u_r[dc * 128:(dc + 1) * 128, q0:q0 + TQ]))
                p.v(lambda v: v.tensor_mul(du_c[:, :], dt_c[:, :], u_c[:, :]))
                for n in range(Nst):
                    col = dc * 16 + n
                    p.s(lambda s, col=col: s.activation(dA[:, 2:TQ + 2], dt_c[:, :], AF.Exp,
                                                        scale=a_s[:, col:col + 1]))
                    p.v(lambda v, col=col: v.tensor_copy(dBu[:, 1:2], hlast[:, col:col + 1]))
                    p.v(lambda v, n=n: v.tensor_mul(dBu[:, 2:TQ + 2], du_c[:, :],
                                                    b_rep[:, n * TQ:(n + 1) * TQ]))
                    p.v(lambda v: v.tensor_tensor_scan(hbuf[:, :], dA[:, :], dBu[:, :],
                                                       0.0, ALU.mult, ALU.add))
                    p.v(lambda v, n=n: v.tensor_mul(hC[:, :], hbuf[:, 2:TQ + 2],
                                                    c_rep[:, n * TQ:(n + 1) * TQ]))
                    p.v(lambda v, col=col: v.tensor_copy(hlast[:, col:col + 1],
                                                         hbuf[:, TQ + 1:TQ + 2]))
                    if n == 0:
                        p.v(lambda v: v.tensor_copy(yacc[:, :], hC[:, :]))
                    else:
                        p.v(lambda v: v.tensor_add(yacc[:, :], yacc[:, :], hC[:, :]))
                p.dma("sync", lambda sy, dc=dc, q0=q0:
                      sy.dma_start(out=y_o[dc * 128:(dc + 1) * 128, q0:q0 + TQ],
                                   in_=yacc[:, :]))
        run_prog(nc, p, "scan")
    return nc


def host_prep_scan(dt_full, u_full, Bm, Cm, A_log, core):
    b, half = core // 2, core % 2
    ch0 = half * 1024
    sl = slice(ch0, ch0 + 1024)
    al = np.asarray(A_log)[sl].reshape(8, 128, 16).transpose(1, 0, 2).reshape(128, 128)
    return {
        "dt_r": np.ascontiguousarray(np.asarray(dt_full[b])[:, sl].T),
        "u_r": np.ascontiguousarray(np.asarray(u_full[b])[:, sl].T),
        "b_r": np.ascontiguousarray(np.asarray(Bm[b]).T),
        "c_r": np.ascontiguousarray(np.asarray(Cm[b]).T),
        "a_log_r": np.ascontiguousarray(al),
        "ones_r_d": np.ones((1, 128), np.float32),
    }





def build_launch_C():
    nc = bass.Bass("TRN2", target_bir_lowering=False)

    def din(name, shape, dt=F32):
        return nc.dram_tensor(name, shape, dt, kind="ExternalInput").ap()

    x_ft = din("x_ft", [E, S])
    x_tk = din("x_tk", [TC, E])
    w_qk = din("w_qk", [E, 2048])
    w_v = din("w_v", [E, E])
    bqk_r = din("bqk_r", [128, 16])
    bv_row = din("bv_row", [1, E])
    w_gate = din("w_gate", [E, 8])
    bg_row = din("bg_row", [1, 8])
    w_exp = din("w_exp", [8 * E, E])
    b_exp = din("b_exp", [8, E])
    g1out_p = din("g1out_p", [16, 1024])
    g2out_p = din("g2out_p", [16, 2048])
    outb_row = din("outb_row", [1, E])
    d_col = din("d_col", [128, 16])
    yssm = din("yssm", [DI, TC])
    u_own = din("u_own", [DI, TC])
    ones_c = din("ones_c", [128, 1])
    ones_r_d = din("ones_r_d", [1, 128])
    identf = din("identf", [128, 128])

    x1_o = nc.dram_tensor("x1_o", [TC, E], F32, kind="ExternalOutput").ap()
    DBG = os.environ.get("K_DEBUG_C", "0") == "1"
    if DBG:
        attn_dbg = nc.dram_tensor("attn_dbg", [TC, E], F32, kind="ExternalOutput").ap()
        moe_dbg = nc.dram_tensor("moe_dbg", [TC, E], F32, kind="ExternalOutput").ap()
        comb_dbg = nc.dram_tensor("comb_dbg", [128, 128], F32, kind="ExternalOutput").ap()
        g_dbg = nc.dram_tensor("g_dbg", [TC, E], F32, kind="ExternalOutput").ap()
        s_dbg = nc.dram_tensor("s_dbg", [TC, E], F32, kind="ExternalOutput").ap()

    scr = nc.dram_tensor("scr", [1024, 2048], BF16)
    wout_scr = nc.dram_tensor("wout_scr", [2048, 1024], BF16)
    k_scr = nc.dram_tensor("k_scr", [1024, S], BF16)
    v_scr = nc.dram_tensor("v_scr", [S, 1040], BF16)
    attn_scr = nc.dram_tensor("attn_scr", [TC, E], F32)

    H = 16

    with contextlib.ExitStack() as st0:
        sb = lambda n, shp, dt: st0.enter_context(nc.sbuf_tensor(n, shp, dt))
        psum = lambda n, shp: st0.enter_context(nc.psum_tensor(n, shp, F32))
        ones_f = sb("ones_f", [128, 1], F32)
        ones_r = sb("ones_r", [1, 128], F32)
        ones_rb = sb("ones_rb", [1, 128], BF16)
        identf_s = sb("identf_s", [128, 128], F32)
        bqk_s = sb("bqk_s", [128, 16], F32)
        d_s = sb("d_s", [128, 16], F32)
        mrow = sb("mrow", [1, 16], F32)
        mneg_col = sb("mneg_col", [128, 1], F32)
        qkabs = sb("qkabs", [128, 128], F32)
        strow = sb("strow", [1, 6 * 512], F32)
        xsq = sb("xsq", [128, 512], F32)
        xstack = contextlib.ExitStack()
        xn_bf = xstack.enter_context(nc.sbuf_tensor("xn_bf", [128, 8 * S], BF16))
        ps_stack = contextlib.ExitStack()
        psx = lambda n, shp: ps_stack.enter_context(nc.psum_tensor(n, shp, F32))
        ps_a = psx("ps_a", [128, 512])
        ps_b = psx("ps_b", [128, 512])
        ps_st = psx("ps_st", [1, 1024])
        ps_r = psx("ps_r", [128, 512])
        ps_m = psx("ps_m", [128, 512])

        p = Prog(nc)
        p.dma("sync", lambda sy: sy.dma_start(out=ones_f[:], in_=ones_c))
        p.dma("sync", lambda sy: sy.dma_start(out=ones_r[:], in_=ones_r_d))
        p.dma("gpsimd", lambda g: g.dma_start(out=ones_rb[:], in_=ones_r_d))
        p.dma("sync", lambda sy: sy.dma_start(out=identf_s[:], in_=identf))
        p.dma("sync", lambda sy: sy.dma_start(out=bqk_s[:], in_=bqk_r))
        p.dma("sync", lambda sy: sy.dma_start(out=d_s[:], in_=d_col))
        with contextlib.ExitStack() as wst:
            g1s = wst.enter_context(nc.sbuf_tensor("g1s_o", [16, 1024], BF16))
            g2s = wst.enter_context(nc.sbuf_tensor("g2s_o", [16, 2048], BF16))
            outs = wst.enter_context(nc.sbuf_tensor("outs_o", [128, 8 * 2048], BF16))
            wps = wst.enter_context(nc.psum_tensor("wps_o", [128, 512], F32))
            p.dma("gpsimd", lambda g: g.dma_start(out=g1s[:], in_=g1out_p))
            p.dma("gpsimd", lambda g: g.dma_start(out=g2s[:], in_=g2out_p))
            for mo, ml in nchunks(1024, 128):
                for no, nl in nchunks(2048, 512):
                    p.t(lambda t, mo=mo, ml=ml, no=no, nl=nl:
                        t.matmul(wps[0:ml, 0:nl], g1s[:, mo:mo + ml], g2s[:, no:no + nl],
                                 start=True, stop=True))
                    p.v(lambda v, mo=mo, ml=ml, no=no, nl=nl:
                        v.tensor_copy(outs[0:ml, (mo // 128) * 2048 + no:(mo // 128) * 2048 + no + nl],
                                      wps[0:ml, 0:nl]))
            for mo, ml in nchunks(1024, 128):
                p.dma("gpsimd", lambda g, mo=mo, ml=ml:
                      g.dma_start(out=scr.ap()[mo:mo + ml, 0:2048],
                                  in_=outs[0:ml, (mo // 128) * 2048:(mo // 128) * 2048 + 2048]))
            for ii in range(32):
                sap = bass.AP(scr, (ii * 32) * 2048, [[32, 64], [2048, 32], [1, 32]])
                dap = bass.AP(wout_scr, ii * 64 * 1024, [[1, 64 * 1024]])
                p.dma("gpsimd", lambda g, sap=sap, dap=dap:
                      g.dma_start(out=dap, in_=sap), fence=True)
            run_prog(nc, p, "c0")

        p = Prog(nc)
        with contextlib.ExitStack() as c1s:
            xw8 = c1s.enter_context(nc.sbuf_tensor("xw8", [128, 8 * 512], F32))
            for tch in range(8):
                c0c = tch * 512
                for kc in range(8):
                    p.dma("sync", lambda sy, kc=kc, c0c=c0c:
                          sy.dma_start(out=xw8[:, kc * 512:(kc + 1) * 512],
                                       in_=x_ft[kc * 128:(kc + 1) * 128, c0c:c0c + 512]))
                for kc in range(8):
                    p.t(lambda t, kc=kc: t.matmul(ps_st[0:1, 0:512], ones_f[:, 0:1],
                                                  xw8[:, kc * 512:(kc + 1) * 512],
                                                  start=(kc == 0), stop=(kc == 7)))
                for kc in range(8):
                    p.s(lambda s, kc=kc: s.activation(xsq[:, :], xw8[:, kc * 512:(kc + 1) * 512],
                                                      AF.Square))
                    p.t(lambda t, kc=kc: t.matmul(ps_st[0:1, 512:1024], ones_f[:, 0:1],
                                                  xsq[:, :], start=(kc == 0), stop=(kc == 7)))
                p.v(lambda v: v.tensor_scalar(strow[0:1, 0:512], ps_st[0:1, 0:512],
                                              1.0 / 1024.0, None, ALU.mult))
                p.v(lambda v: v.tensor_scalar(strow[0:1, 512:1024], ps_st[0:1, 512:1024],
                                              1.0 / 1024.0, None, ALU.mult))
                p.v(lambda v: v.tensor_mul(strow[0:1, 1024:1536], strow[0:1, 0:512],
                                           strow[0:1, 0:512]))
                p.v(lambda v: v.tensor_sub(strow[0:1, 1024:1536], strow[0:1, 512:1024],
                                           strow[0:1, 1024:1536]))
                p.v(lambda v: v.tensor_scalar(strow[0:1, 1024:1536], strow[0:1, 1024:1536],
                                              1e-5, None, ALU.add))
                p.s(lambda s: s.activation(strow[0:1, 1536:2048], strow[0:1, 1024:1536], AF.Ln))
                p.s(lambda s: s.activation(strow[0:1, 1536:2048], strow[0:1, 1536:2048],
                                           AF.Exp, scale=-0.5))
                p.v(lambda v: v.tensor_mul(strow[0:1, 2048:2560], strow[0:1, 0:512],
                                           strow[0:1, 1536:2048]))
                p.v(lambda v: v.tensor_scalar(strow[0:1, 2048:2560], strow[0:1, 2048:2560],
                                              -1.0, None, ALU.mult))
                p.t(lambda t: t.matmul(ps_r[:, 0:512], ones_r[0:1, :],
                                       strow[0:1, 1536:2048], start=True, stop=True))
                p.t(lambda t: t.matmul(ps_m[:, 0:512], ones_r[0:1, :],
                                       strow[0:1, 2048:2560], start=True, stop=True))
                for kc in range(8):
                    p.v(lambda v, kc=kc: v.tensor_mul(xw8[:, kc * 512:(kc + 1) * 512],
                                                      xw8[:, kc * 512:(kc + 1) * 512],
                                                      ps_r[:, 0:512]))
                    p.v(lambda v, kc=kc, c0c=c0c:
                        v.tensor_tensor(xn_bf[:, kc * S + c0c:kc * S + c0c + 512],
                                        xw8[:, kc * 512:(kc + 1) * 512],
                                        ps_m[:, 0:512], ALU.add))
            run_prog(nc, p, "c1")

        qstack = contextlib.ExitStack()
        q_bf = qstack.enter_context(nc.sbuf_tensor("q_bf", [128, 8 * TC], BF16))
        p = Prog(nc)
        with contextlib.ExitStack() as c2s:
            sb2 = lambda n, shp, dt: c2s.enter_context(nc.sbuf_tensor(n, shp, dt))
            wqk_s = sb2("wqk_s", [128, 8 * 2048], BF16)
            wv_s = sb2("wv_s", [128, 8 * 1024], BF16)
            bv_b = sb2("bv_b", [1, E], BF16)
            kb = sb2("kb", [128, 512], BF16)
            vtile = sb2("vtile", [128, 1040], BF16)
            for kc in range(8):
                p.dma("gpsimd", lambda g, kc=kc:
                      g.dma_start(out=wqk_s[:, kc * 2048:(kc + 1) * 2048],
                                  in_=w_qk[kc * 128:(kc + 1) * 128, :]))
                p.dma("gpsimd", lambda g, kc=kc:
                      g.dma_start(out=wv_s[:, kc * 1024:(kc + 1) * 1024],
                                  in_=w_v[kc * 128:(kc + 1) * 128, :]))
            p.dma("gpsimd", lambda g: g.dma_start(out=bv_b[:], in_=bv_row))
            for tch in range(8):
                c0c = tch * 512
                for oc in range(8):
                    ps = ps_a if oc % 2 == 0 else ps_b
                    for kc in range(8):
                        p.t(lambda t, oc=oc, kc=kc, ps=ps, c0c=c0c:
                            t.matmul(ps[:, 0:512],
                                     wqk_s[:, kc * 2048 + 1024 + oc * 128: kc * 2048 + 1024 + oc * 128 + 128],
                                     xn_bf[:, kc * S + c0c:kc * S + c0c + 512],
                                     start=(kc == 0), stop=(kc == 7)))
                    p.v(lambda v, oc=oc, ps=ps: v.tensor_scalar(kb[:, :], ps[:, 0:512],
                                                                bqk_s[:, 8 + oc:9 + oc],
                                                                None, ALU.add))
                    p.v(lambda v, oc=oc, tch=tch:
                        v.reduce_max(qkabs[:, tch * 8 + oc:tch * 8 + oc + 1], kb[:, :],
                                     axis=AX, apply_absolute_value=True))
                    p.dma("sync", lambda sy, oc=oc, c0c=c0c:
                          sy.dma_start(out=k_scr.ap()[oc * 128:(oc + 1) * 128, c0c:c0c + 512],
                                       in_=kb[:, :]))
            for tt in range(32):
                t0c = tt * 128
                for half in range(2):
                    ps = ps_a if half == 0 else ps_b
                    for kc in range(8):
                        p.t(lambda t, tt=tt, kc=kc, half=half, ps=ps, t0c=t0c:
                            t.matmul(ps[:, 0:512],
                                     xn_bf[:, kc * S + t0c:kc * S + t0c + 128],
                                     wv_s[:, kc * 1024 + half * 512: kc * 1024 + half * 512 + 512],
                                     start=(kc == 0), stop=False))
                    p.t(lambda t, half=half, ps=ps, t0c=t0c:
                        t.matmul(ps[:, 0:512], ones_rb[0:1, 0:128],
                                 bv_b[0:1, half * 512:half * 512 + 512],
                                 start=False, stop=True))
                    p.v(lambda v, half=half, ps=ps:
                        v.tensor_copy(
                            vtile[:, half * 8 * 65:half * 8 * 65 + 520]
                            .rearrange("p (h c) -> p h c", h=8, c=65)[:, :, 0:64],
                            ps[:, 0:512].rearrange("p (h c) -> p h c", h=8, c=64)))
                p.v(lambda v: v.memset(
                    vtile[:, :].rearrange("p (h c) -> p h c", h=16, c=65)[:, :, 64:65], 1.0))
                p.dma("sync", lambda sy, tt=tt:
                      sy.dma_start(out=v_scr.ap()[tt * 128:(tt + 1) * 128, :], in_=vtile[:, :]))
            for tch in range(4):
                c0c = tch * 512
                for oc in range(8):
                    ps = ps_a if oc % 2 == 0 else ps_b
                    for kc in range(8):
                        p.t(lambda t, oc=oc, kc=kc, ps=ps, c0c=c0c:
                            t.matmul(ps[:, 0:512],
                                     wqk_s[:, kc * 2048 + oc * 128: kc * 2048 + oc * 128 + 128],
                                     xn_bf[:, kc * S + c0c:kc * S + c0c + 512],
                                     start=(kc == 0), stop=(kc == 7)))
                    p.v(lambda v, oc=oc, ps=ps, c0c=c0c:
                        v.tensor_scalar(q_bf[:, oc * TC + c0c:oc * TC + c0c + 512],
                                        ps[:, 0:512], bqk_s[:, oc:oc + 1], None, ALU.add))
                    p.v(lambda v, oc=oc, tch=tch, c0c=c0c:
                        v.reduce_max(qkabs[:, 96 + tch * 8 + oc:96 + tch * 8 + oc + 1],
                                     q_bf[:, oc * TC + c0c:oc * TC + c0c + 512],
                                     axis=AX, apply_absolute_value=True))
            p.v(lambda v: v.reduce_max(qkabs[:, 64:65], qkabs[:, 0:64], axis=AX))
            p.v(lambda v: v.reduce_max(qkabs[:, 65:66], qkabs[:, 96:128], axis=AX))
            kqrow = c2s.enter_context(nc.sbuf_tensor("kqrow", [1, 256], F32))
            p.dma("gpsimd", lambda g: g.dma_start(out=kqrow[0:1, 0:128], in_=qkabs[:, 64:65]))
            p.dma("gpsimd", lambda g: g.dma_start(out=kqrow[0:1, 128:256], in_=qkabs[:, 65:66]))
            p.v(lambda v: v.reduce_max(mrow[0:1, 0:1], kqrow[0:1, 0:128], axis=AX))
            p.v(lambda v: v.reduce_max(mrow[0:1, 1:2], kqrow[0:1, 128:256], axis=AX))
            p.v(lambda v: v.tensor_mul(mrow[0:1, 2:3], mrow[0:1, 0:1], mrow[0:1, 1:2]))
            p.v(lambda v: v.tensor_scalar(mrow[0:1, 2:3], mrow[0:1, 2:3], -8.0, None, ALU.mult))
            p.t(lambda t: t.matmul(ps_r[:, 0:1], ones_r[0:1, :], mrow[0:1, 2:3],
                                   start=True, stop=True))
            p.v(lambda v: v.tensor_copy(mneg_col[:, :], ps_r[:, 0:1]))
            run_prog(nc, p, "c2")

        ps_stack.close()
        QTB = 512
        KSC = 1024
        p = Prog(nc)
        with contextlib.ExitStack() as c4s:
            sb4 = lambda n, shp, dt: c4s.enter_context(nc.sbuf_tensor(n, shp, dt))
            k_sc = sb4("k_sc", [128, 8 * KSC], BF16)
            v_sc = sb4("v_sc", [128, 8 * 1040], BF16)
            p_bf = sb4("p_bf", [128, 4 * 512], BF16)
            av_acc = sb4("av_acc", [65, H * QTB], F32)
            att_t = sb4("att_t", [128, 1024], F32)
            sums_c = sb4("sums_c", [128, 32], F32)
            ps_sc = [c4s.enter_context(nc.psum_tensor(f"ps_sc{i}", [128, 512], F32))
                     for i in range(4)]
            ps_av = c4s.enter_context(nc.psum_tensor("ps_av", [65, 512], F32))
            ps_tr = c4s.enter_context(nc.psum_tensor("ps_tr", [128, 65], F32))
            for qtb in range(4):
                qb0 = qtb * QTB
                for ksc in range(4):
                    kb0 = ksc * KSC
                    for hp in range(8):
                        p.dma("gpsimd", lambda g, hp=hp, kb0=kb0:
                              g.dma_start(out=k_sc[:, hp * KSC:(hp + 1) * KSC],
                                          in_=k_scr.ap()[hp * 128:(hp + 1) * 128, kb0:kb0 + KSC]))
                    for kt in range(8):
                        p.dma("gpsimd", lambda g, kt=kt, kb0=kb0:
                              g.dma_start(out=v_sc[:, kt * 1040:(kt + 1) * 1040],
                                          in_=v_scr.ap()[kb0 + kt * 128:kb0 + (kt + 1) * 128, :]))
                    for h in range(H):
                        hr, hc = (h % 2) * 64, h // 2
                        for ktg in range(2):
                            for j in range(4):
                                kt = ktg * 4 + j
                                p.t(lambda t, hr=hr, hc=hc, kt=kt, j=j, qb0=qb0:
                                    t.matmul(ps_sc[j][:, 0:512],
                                             k_sc[hr:hr + 64,
                                                  hc * KSC + kt * 128:hc * KSC + kt * 128 + 128],
                                             q_bf[hr:hr + 64,
                                                  hc * TC + qb0:hc * TC + qb0 + QTB],
                                             start=True, stop=True))
                            for j in range(4):
                                p.s(lambda s, j=j: s.activation(p_bf[:, j * 512:(j + 1) * 512],
                                                                ps_sc[j][:, 0:512], AF.Exp,
                                                                bias=mneg_col[:, 0:1], scale=0.125))
                            for j in range(4):
                                kt = ktg * 4 + j
                                p.t(lambda t, kt=kt, j=j, h=h:
                                    t.matmul(ps_av[:, 0:512],
                                             v_sc[:, kt * 1040 + h * 65:kt * 1040 + h * 65 + 65],
                                             p_bf[:, j * 512:(j + 1) * 512],
                                             start=(kt == 0), stop=(kt == 7)))
                        if ksc == 0:
                            p.v(lambda v, h=h: v.tensor_copy(av_acc[:, h * QTB:(h + 1) * QTB],
                                                             ps_av[0:65, 0:512]))
                        else:
                            p.v(lambda v, h=h: v.tensor_add(av_acc[:, h * QTB:(h + 1) * QTB],
                                                            av_acc[:, h * QTB:(h + 1) * QTB],
                                                            ps_av[0:65, 0:512]))
                for ttl in range(4):
                    for h in range(H):
                        p.t(lambda t, h=h, ttl=ttl:
                            t.transpose(ps_tr[:, 0:65],
                                        av_acc[0:65, h * QTB + ttl * 128:h * QTB + ttl * 128 + 128],
                                        identf_s[0:65, 0:65]))
                        p.v(lambda v, h=h: v.tensor_copy(sums_c[:, h:h + 1], ps_tr[:, 64:65]))
                        p.v(lambda v, h=h: v.tensor_copy(att_t[:, h * 64:(h + 1) * 64],
                                                         ps_tr[:, 0:64]))
                    p.v(lambda v: v.tensor_scalar(att_t[:, 0:512], att_t[:, 0:512],
                                                  1.0, None, ALU.mult))
                    p.v(lambda v: v.reciprocal(sums_c[:, 16:32], sums_c[:, 0:16]))
                    p.v(lambda v: v.tensor_scalar(att_t[:, 512:1024], att_t[:, 512:1024],
                                                  1.0, None, ALU.mult))
                    for h in range(H):
                        p.v(lambda v, h=h: v.tensor_scalar(att_t[:, h * 64:(h + 1) * 64],
                                                           att_t[:, h * 64:(h + 1) * 64],
                                                           sums_c[:, 16 + h:17 + h], None, ALU.mult))
                    p.dma("sync", lambda sy, qb0=qb0, ttl=ttl:
                          sy.dma_start(out=attn_scr.ap()[qb0 + ttl * 128:qb0 + (ttl + 1) * 128, :],
                                       in_=att_t[:, :]))
                    if DBG:
                        p.dma("sync", lambda sy, qb0=qb0, ttl=ttl:
                              sy.dma_start(out=attn_dbg[qb0 + ttl * 128:qb0 + (ttl + 1) * 128, :],
                                           in_=att_t[:, :]))
            run_prog(nc, p, "c4")
        qstack.close()

        p = Prog(nc)
        with contextlib.ExitStack() as c5s:
            sb5 = lambda n, shp, dt: c5s.enter_context(nc.sbuf_tensor(n, shp, dt))
            wg_b = sb5("wg_b", [128, 8 * 8], BF16)
            bg_b = sb5("bg_b", [1, 8], BF16)
            we_sb = sb5("we_sb", [128, 8 * 1024], BF16)
            bexp_b = sb5("bexp_b", [1, 8 * 1024], BF16)
            comb_q = sb5("comb_q", [128, 16 * 8], F32)
            moe_q = sb5("moe_q", [128, 4 * 1024], F32)
            att5 = sb5("att5", [128, 1024], F32)
            ps_g = c5s.enter_context(nc.psum_tensor("ps_g", [128, 512], F32))
            ps_e0 = c5s.enter_context(nc.psum_tensor("ps_e0", [128, 1024], F32))
            ps_e1 = c5s.enter_context(nc.psum_tensor("ps_e1", [128, 1024], F32))
            for kc in range(8):
                p.dma("gpsimd", lambda g, kc=kc:
                      g.dma_start(out=wg_b[:, kc * 8:(kc + 1) * 8],
                                  in_=w_gate[kc * 128:(kc + 1) * 128, :]))
            p.dma("gpsimd", lambda g: g.dma_start(out=bg_b[:], in_=bg_row))
            for e in range(8):
                p.dma("gpsimd", lambda g, e=e:
                      g.dma_start(out=bexp_b[0:1, e * 1024:(e + 1) * 1024],
                                  in_=b_exp[e:e + 1, :]))
            gate_all = sb5("gate_all", [128, 128], F32)
            e_all = sb5("e_all", [128, 128], F32)
            ind_all = sb5("ind_all", [128, 128], F32)
            msk_all = sb5("msk_all", [128, 128], F32)
            stat16 = sb5("stat16", [128, 5 * 16], F32)
            for tt in range(16):
                tt0 = tt * 128
                for kc in range(8):
                    p.t(lambda t, kc=kc, tt0=tt0:
                        t.matmul(ps_g[:, 0:8], xn_bf[:, kc * S + tt0:kc * S + tt0 + 128],
                                 wg_b[:, kc * 8:(kc + 1) * 8], start=(kc == 0), stop=False))
                p.t(lambda t: t.matmul(ps_g[:, 0:8], ones_rb[0:1, 0:128],
                                       bg_b[0:1, :], start=False, stop=True))
                p.v(lambda v, tt=tt: v.tensor_copy(gate_all[:, tt * 8:(tt + 1) * 8],
                                                   ps_g[:, 0:8]))
            g3 = gate_all[:, :].rearrange("p (t e) -> p t e", t=16, e=8)
            e3 = e_all[:, :].rearrange("p (t e) -> p t e", t=16, e=8)
            m3 = msk_all[:, :].rearrange("p (t e) -> p t e", t=16, e=8)
            p.v(lambda v: v.reduce_max(stat16[:, 0:16], g3, axis=AX))
            p.v(lambda v: v.tensor_scalar(gate_all[:, :], gate_all[:, :], 1.0, None, ALU.mult))
            p.v(lambda v: v.tensor_scalar(stat16[:, 16:32], stat16[:, 0:16], -1.0, None, ALU.mult))
            for tt in range(16):
                p.s(lambda s, tt=tt: s.activation(e_all[:, tt * 8:(tt + 1) * 8],
                                                  gate_all[:, tt * 8:(tt + 1) * 8], AF.Exp,
                                                  bias=stat16[:, 16 + tt:17 + tt]))
            p.v(lambda v: v.reduce_sum(stat16[:, 32:48], e3, axis=AX))
            p.v(lambda v: v.reduce_max(stat16[:, 48:64], e3, axis=AX))
            p.v(lambda v: v.reciprocal(stat16[:, 64:80], stat16[:, 32:48]))
            p.v(lambda v: v.tensor_scalar(e_all[:, :], e_all[:, :], 1.0, None, ALU.mult))
            for tt in range(16):
                p.v(lambda v, tt=tt: v.tensor_scalar(ind_all[:, tt * 8:(tt + 1) * 8],
                                                     e_all[:, tt * 8:(tt + 1) * 8],
                                                     stat16[:, 48 + tt:49 + tt], None,
                                                     ALU.is_equal))
            p.v(lambda v: v.scalar_tensor_tensor(msk_all[:, :], ind_all[:, :], -1e30,
                                                 e_all[:, :], ALU.mult, ALU.add))
            p.v(lambda v: v.reduce_max(stat16[:, 48:64], m3, axis=AX))
            p.v(lambda v: v.tensor_scalar(msk_all[:, :], msk_all[:, :], 1.0, None, ALU.mult))
            for tt in range(16):
                p.v(lambda v, tt=tt: v.tensor_scalar(msk_all[:, tt * 8:(tt + 1) * 8],
                                                     msk_all[:, tt * 8:(tt + 1) * 8],
                                                     stat16[:, 48 + tt:49 + tt], None,
                                                     ALU.is_equal))
            p.v(lambda v: v.tensor_add(ind_all[:, :], ind_all[:, :], msk_all[:, :]))
            p.v(lambda v: v.tensor_mul(comb_q[:, :], e_all[:, :], ind_all[:, :]))
            for tt in range(16):
                p.v(lambda v, tt=tt: v.tensor_scalar(comb_q[:, tt * 8:(tt + 1) * 8],
                                                     comb_q[:, tt * 8:(tt + 1) * 8],
                                                     stat16[:, 64 + tt:65 + tt], None,
                                                     ALU.mult))
            for qq in range(4):
                for e in range(8):
                    for kc in range(8):
                        p.dma("gpsimd", lambda g, e=e, kc=kc:
                              g.dma_start(out=we_sb[:, kc * 1024:(kc + 1) * 1024],
                                          in_=w_exp[e * 1024 + kc * 128:e * 1024 + (kc + 1) * 128, :]))
                    for tl in range(4):
                        tt = qq * 4 + tl
                        tt0 = tt * 128
                        pse = ps_e0 if tl % 2 == 0 else ps_e1
                        for kc in range(8):
                            p.t(lambda t, kc=kc, tt0=tt0, pse=pse:
                                t.matmul(pse[:, 0:512],
                                         xn_bf[:, kc * S + tt0:kc * S + tt0 + 128],
                                         we_sb[:, kc * 1024:kc * 1024 + 512],
                                         start=(kc == 0), stop=False))
                        p.t(lambda t, e=e, pse=pse: t.matmul(pse[:, 0:512], ones_rb[0:1, 0:128],
                                                             bexp_b[0:1, e * 1024:e * 1024 + 512],
                                                             start=False, stop=True))
                        for kc in range(8):
                            p.t(lambda t, kc=kc, tt0=tt0, pse=pse:
                                t.matmul(pse[:, 512:1024],
                                         xn_bf[:, kc * S + tt0:kc * S + tt0 + 128],
                                         we_sb[:, kc * 1024 + 512:kc * 1024 + 1024],
                                         start=(kc == 0), stop=False))
                        p.t(lambda t, e=e, pse=pse: t.matmul(pse[:, 512:1024], ones_rb[0:1, 0:128],
                                                             bexp_b[0:1, e * 1024 + 512:(e + 1) * 1024],
                                                             start=False, stop=True))
                        for half in range(2):
                            if e == 0:
                                p.v(lambda v, tl=tl, half=half, tt=tt, pse=pse:
                                    v.tensor_scalar(moe_q[:, tl * 1024 + half * 512:tl * 1024 + half * 512 + 512],
                                                    pse[:, half * 512:(half + 1) * 512],
                                                    comb_q[:, tt * 8:tt * 8 + 1], None, ALU.mult))
                            else:
                                p.v(lambda v, tl=tl, half=half, tt=tt, e=e, pse=pse:
                                    v.scalar_tensor_tensor(
                                        moe_q[:, tl * 1024 + half * 512:tl * 1024 + half * 512 + 512],
                                        pse[:, half * 512:(half + 1) * 512],
                                        comb_q[:, tt * 8 + e:tt * 8 + e + 1],
                                        moe_q[:, tl * 1024 + half * 512:tl * 1024 + half * 512 + 512],
                                        ALU.mult, ALU.add))
                for tl in range(4):
                    tt = qq * 4 + tl
                    if DBG:
                        p.dma("sync", lambda sy, tt=tt, tl=tl:
                              sy.dma_start(out=moe_dbg[tt * 128:(tt + 1) * 128, :],
                                           in_=moe_q[:, tl * 1024:(tl + 1) * 1024]))
                    p.dma("sync", lambda sy, tt=tt:
                          sy.dma_start(out=att5[:, :],
                                       in_=attn_scr.ap()[tt * 128:(tt + 1) * 128, :]))
                    p.v(lambda v, tl=tl: v.tensor_add(att5[:, :], att5[:, :],
                                                      moe_q[:, tl * 1024:(tl + 1) * 1024]))
                    p.s(lambda s: s.activation(att5[:, :], att5[:, :], AF.Silu))
                    p.dma("sync", lambda sy, tt=tt:
                          sy.dma_start(out=attn_scr.ap()[tt * 128:(tt + 1) * 128, :],
                                       in_=att5[:, :]))
                    if DBG:
                        p.dma("sync", lambda sy, tt=tt:
                              sy.dma_start(out=g_dbg[tt * 128:(tt + 1) * 128, :],
                                           in_=att5[:, :]))
            if DBG:
                p.dma("sync", lambda sy: sy.dma_start(out=comb_dbg, in_=comb_q[:, :]))
            run_prog(nc, p, "c5")
        xstack.close()

        p = Prog(nc)
        with contextlib.ExitStack() as c6s:
            sb6 = lambda n, shp, dt: c6s.enter_context(nc.sbuf_tensor(n, shp, dt))
            ssm_bf = sb6("ssm_bf", [128, 16 * TC], BF16)
            w_out6 = sb6("w_out6", [128, 16 * 1024], BF16)
            outb_b = sb6("outb_b", [1, E], BF16)
            ych = sb6("ych", [128, TC], F32)
            uch = sb6("uch", [128, TC], F32)
            g6 = sb6("g6", [128, 1024], F32)
            xk6 = sb6("xk6", [128, 1024], F32)
            x16 = sb6("x16", [128, 1024], F32)
            ps_o0 = c6s.enter_context(nc.psum_tensor("ps_o0", [128, 1024], F32))
            ps_o1 = c6s.enter_context(nc.psum_tensor("ps_o1", [128, 1024], F32))
            for kc in range(16):
                p.dma("gpsimd", lambda g, kc=kc:
                      g.dma_start(out=w_out6[:, kc * 1024:(kc + 1) * 1024],
                                  in_=wout_scr.ap()[kc * 128:(kc + 1) * 128, :]))
            p.dma("gpsimd", lambda g: g.dma_start(out=outb_b[:], in_=outb_row))
            for fc in range(16):
                p.dma("sync", lambda sy, fc=fc:
                      sy.dma_start(out=ych[:, :], in_=yssm[fc * 128:(fc + 1) * 128, :]))
                p.dma("sync", lambda sy, fc=fc:
                      sy.dma_start(out=uch[:, :], in_=u_own[fc * 128:(fc + 1) * 128, :]))
                p.v(lambda v, fc=fc: v.scalar_tensor_tensor(ssm_bf[:, fc * TC:(fc + 1) * TC],
                                                            uch[:, :], d_s[:, fc:fc + 1],
                                                            ych[:, :], ALU.mult, ALU.add))
            for tt in range(16):
                tt0 = tt * 128
                pso = ps_o0 if tt % 2 == 0 else ps_o1
                for half in range(2):
                    for fc in range(16):
                        p.t(lambda t, fc=fc, tt0=tt0, half=half, pso=pso:
                            t.matmul(pso[:, half * 512:(half + 1) * 512],
                                     ssm_bf[:, fc * TC + tt0:fc * TC + tt0 + 128],
                                     w_out6[:, fc * 1024 + half * 512:fc * 1024 + half * 512 + 512],
                                     start=(fc == 0), stop=False))
                    p.t(lambda t, half=half, pso=pso:
                        t.matmul(pso[:, half * 512:(half + 1) * 512], ones_rb[0:1, 0:128],
                                 outb_b[0:1, half * 512:(half + 1) * 512],
                                 start=False, stop=True))
                p.dma("sync", lambda sy, tt=tt:
                      sy.dma_start(out=g6[:, :], in_=attn_scr.ap()[tt * 128:(tt + 1) * 128, :]))
                p.dma("sync", lambda sy, tt=tt:
                      sy.dma_start(out=xk6[:, :], in_=x_tk[tt * 128:(tt + 1) * 128, :]))
                if DBG:
                    for half in range(2):
                        p.v(lambda v, half=half, pso=pso:
                            v.tensor_copy(x16[:, half * 512:(half + 1) * 512],
                                          pso[:, half * 512:(half + 1) * 512]))
                    p.dma("sync", lambda sy, tt=tt:
                          sy.dma_start(out=s_dbg[tt * 128:(tt + 1) * 128, :], in_=x16[:, :]))
                for half in range(2):
                    p.v(lambda v, half=half, pso=pso:
                        v.tensor_mul(x16[:, half * 512:(half + 1) * 512],
                                     pso[:, half * 512:(half + 1) * 512],
                                     g6[:, half * 512:(half + 1) * 512]))
                p.v(lambda v: v.tensor_add(x16[:, :], x16[:, :], xk6[:, :]))
                p.dma("sync", lambda sy, tt=tt:
                      sy.dma_start(out=x1_o[tt * 128:(tt + 1) * 128, :], in_=x16[:, :]))
            run_prog(nc, p, "c6")
    return nc


def host_prep_C(inp, x1dummy, yssm_full, u_cores, core):
    b, h = core // 2, core % 2
    t0 = h * TC
    x = np.asarray(inp["L0_attn_x"][b])
    xT = np.ascontiguousarray(x.T)
    x_ft = np.concatenate([xT[:, t0:t0 + TC], xT[:, TC - t0:2 * TC - t0]], axis=1)
    x_tk = np.ascontiguousarray(x[t0:t0 + TC, :])
    W_qkv = np.asarray(inp["L0_ffn_W_qkv"])
    b_qkv = np.asarray(inp["L0_ffn_b_qkv"])
    col = lambda a, n: np.ascontiguousarray(np.asarray(a).reshape(n, 128).T)
    pr = lambda g: np.ascontiguousarray(g.transpose(2, 0, 1).reshape(16, -1))
    rs = lambda g: np.ascontiguousarray(g.reshape(16, -1))
    yssm_c = np.ascontiguousarray(yssm_full[b, t0:t0 + TC, :].T)
    return {
        "x_ft": np.ascontiguousarray(x_ft),
        "x_tk": x_tk,
        "w_qk": np.ascontiguousarray(W_qkv[:, 0:2048]),
        "w_v": np.ascontiguousarray(W_qkv[:, 2048:3072]),
        "bqk_r": col(b_qkv[0:2048], 16),
        "bv_row": np.ascontiguousarray(b_qkv[2048:3072].reshape(1, E)),
        "w_gate": np.ascontiguousarray(np.asarray(inp["L0_ffn_W_gate"])),
        "bg_row": np.ascontiguousarray(np.asarray(inp["L0_ffn_b_gate"]).reshape(1, 8)),
        "w_exp": np.ascontiguousarray(np.asarray(inp["L0_ffn_W_experts"]).reshape(8 * E, E)),
        "b_exp": np.ascontiguousarray(np.asarray(inp["L0_ffn_b_experts"])),
        "g1out_p": pr(np.asarray(inp["L0_attn_out_g1"])),
        "g2out_p": rs(np.asarray(inp["L0_attn_out_g2"])),
        "outb_row": np.ascontiguousarray(np.asarray(inp["L0_attn_out_b"]).reshape(1, E)),
        "d_col": col(inp["L0_attn_D_param"], 16),
        "yssm": yssm_c,
        "u_own": np.ascontiguousarray(np.asarray(u_cores[core])),
        "ones_c": np.ones((128, 1), np.float32),
        "ones_r_d": np.ones((1, 128), np.float32),
        "identf": np.eye(128, dtype=np.float32),
    }




TW1 = TC + 2


def build_launch_D():
    nc = bass.Bass("TRN2", target_bir_lowering=False)

    def din(name, shape, dt=F32):
        return nc.dram_tensor(name, shape, dt, kind="ExternalInput").ap()

    xw1 = din("xw1", [E, TW1])
    conv_wt = din("conv_wt", [128, 24 * 16 * 128])
    convb_c = din("convb_c", [128, 16])
    in_wt = din("in_wt", [128, 16 * 16 * 128])
    inb_c = din("inb_c", [128, 16])
    dt_wt = din("dt_wt", [128, 16 * 16 * 128])
    dtb_c = din("dtb_c", [128, 16])
    bp_t = din("bp_t", [DI, Nst])
    cp_t = din("cp_t", [DI, Nst])
    bpb_c = din("bpb_c", [Nst, 1])
    cpb_c = din("cpb_c", [Nst, 1])
    ones_c = din("ones_c", [128, 1])
    ones_r_d = din("ones_r_d", [1, 128])

    dt_o = nc.dram_tensor("dt_o", [DI, TC], F32, kind="ExternalOutput").ap()
    u_o = nc.dram_tensor("u_o", [DI, TC], F32, kind="ExternalOutput").ap()
    b_o = nc.dram_tensor("b_o", [Nst, TC], F32, kind="ExternalOutput").ap()
    c_o = nc.dram_tensor("c_o", [Nst, TC], F32, kind="ExternalOutput").ap()

    with contextlib.ExitStack() as st0:
        sb = lambda n, shp, dt: st0.enter_context(nc.sbuf_tensor(n, shp, dt))
        ones_f = sb("ones_f", [128, 1], F32)
        ones_r = sb("ones_r", [1, 128], F32)
        convb_s = sb("convb_s", [128, 16], F32)
        inb_s = sb("inb_s", [128, 16], F32)
        dtb_s = sb("dtb_s", [128, 16], F32)
        bpb_s = sb("bpb_s", [16, 1], F32)
        cpb_s = sb("cpb_s", [16, 1], F32)
        strow = sb("strow", [1, 6 * 512], F32)
        xsq = sb("xsq", [128, 514], F32)
        wstr = sb("wstr", [128, 2 * 3072], BF16)
        t1 = sb("t1", [128, 512], F32)
        t2 = sb("t2", [128, 512], F32)
        u_s = sb("u_s", [128, 512], F32)
        bc_s = sb("bc_s", [64, 2048], F32)
        u_sl = sb("u_sl", [128, 16 * 512], BF16)
        xc_bf = st0.enter_context(nc.sbuf_tensor("xc_bf", [128, 16 * TC], BF16))
        xst = contextlib.ExitStack()
        xn1_bf = xst.enter_context(nc.sbuf_tensor("xn1_bf", [128, 8 * TW1], BF16))
        ps_stack = contextlib.ExitStack()
        psx = lambda n, shp: ps_stack.enter_context(nc.psum_tensor(n, shp, F32))
        ps_a = psx("ps_a", [128, 512])
        ps_b = psx("ps_b", [128, 512])
        ps_st = psx("ps_st", [1, 1024])
        ps_r = psx("ps_r", [128, 514])
        ps_m = psx("ps_m", [128, 514])

        p = Prog(nc)
        for dst, srcd in ((ones_f, ones_c), (convb_s, convb_c), (inb_s, inb_c),
                          (dtb_s, dtb_c), (bpb_s, bpb_c), (cpb_s, cpb_c)):
            p.dma("sync", lambda sy, dst=dst, srcd=srcd: sy.dma_start(out=dst[:], in_=srcd))
        p.dma("sync", lambda sy: sy.dma_start(out=ones_r[:], in_=ones_r_d))

        with contextlib.ExitStack() as d1s:
            xw8 = d1s.enter_context(nc.sbuf_tensor("xw8", [128, 8 * 514], F32))
            for c0c, wl in nchunks(TW1, 512):
                for kc in range(8):
                    p.dma("sync", lambda sy, kc=kc, c0c=c0c, wl=wl:
                          sy.dma_start(out=xw8[:, kc * 514:kc * 514 + wl],
                                       in_=xw1[kc * 128:(kc + 1) * 128, c0c:c0c + wl]))
                for kc in range(8):
                    p.t(lambda t, kc=kc, wl=wl: t.matmul(ps_st[0:1, 0:wl], ones_f[:, 0:1],
                                                         xw8[:, kc * 514:kc * 514 + wl],
                                                         start=(kc == 0), stop=(kc == 7)))
                for kc in range(8):
                    p.s(lambda s, kc=kc, wl=wl: s.activation(xsq[:, 0:wl],
                                                             xw8[:, kc * 514:kc * 514 + wl],
                                                             AF.Square))
                    p.t(lambda t, kc=kc, wl=wl: t.matmul(ps_st[0:1, 512:512 + wl], ones_f[:, 0:1],
                                                         xsq[:, 0:wl], start=(kc == 0),
                                                         stop=(kc == 7)))
                p.v(lambda v, wl=wl: v.tensor_scalar(strow[0:1, 0:wl], ps_st[0:1, 0:wl],
                                                     1.0 / 1024.0, None, ALU.mult))
                p.v(lambda v, wl=wl: v.tensor_scalar(strow[0:1, 512:512 + wl],
                                                     ps_st[0:1, 512:512 + wl],
                                                     1.0 / 1024.0, None, ALU.mult))
                p.v(lambda v, wl=wl: v.tensor_mul(strow[0:1, 1024:1024 + wl], strow[0:1, 0:wl],
                                                  strow[0:1, 0:wl]))
                p.v(lambda v, wl=wl: v.tensor_sub(strow[0:1, 1024:1024 + wl],
                                                  strow[0:1, 512:512 + wl],
                                                  strow[0:1, 1024:1024 + wl]))
                p.v(lambda v, wl=wl: v.tensor_scalar(strow[0:1, 1024:1024 + wl],
                                                     strow[0:1, 1024:1024 + wl],
                                                     1e-5, None, ALU.add))
                p.s(lambda s, wl=wl: s.activation(strow[0:1, 1536:1536 + wl],
                                                  strow[0:1, 1024:1024 + wl], AF.Ln))
                p.s(lambda s, wl=wl: s.activation(strow[0:1, 1536:1536 + wl],
                                                  strow[0:1, 1536:1536 + wl], AF.Exp, scale=-0.5))
                p.v(lambda v, wl=wl: v.tensor_mul(strow[0:1, 2048:2048 + wl], strow[0:1, 0:wl],
                                                  strow[0:1, 1536:1536 + wl]))
                p.v(lambda v, wl=wl: v.tensor_scalar(strow[0:1, 2048:2048 + wl],
                                                     strow[0:1, 2048:2048 + wl],
                                                     -1.0, None, ALU.mult))
                p.t(lambda t, wl=wl: t.matmul(ps_r[:, 0:wl], ones_r[0:1, :],
                                              strow[0:1, 1536:1536 + wl], start=True, stop=True))
                p.t(lambda t, wl=wl: t.matmul(ps_m[:, 0:wl], ones_r[0:1, :],
                                              strow[0:1, 2048:2048 + wl], start=True, stop=True))
                for kc in range(8):
                    p.v(lambda v, kc=kc, wl=wl: v.tensor_mul(xw8[:, kc * 514:kc * 514 + wl],
                                                             xw8[:, kc * 514:kc * 514 + wl],
                                                             ps_r[:, 0:wl]))
                    p.v(lambda v, kc=kc, c0c=c0c, wl=wl:
                        v.tensor_tensor(xn1_bf[:, kc * TW1 + c0c:kc * TW1 + c0c + wl],
                                        xw8[:, kc * 514:kc * 514 + wl],
                                        ps_m[:, 0:wl], ALU.add))
            run_prog(nc, p, "d1")

        p = Prog(nc)
        for oc in range(16):
            p.dma("gpsimd", lambda g, oc=oc:
                  g.dma_start(out=wstr[:, (oc % 2) * 3072:(oc % 2) * 3072 + 3072],
                              in_=conv_wt[:, oc * 3072:(oc + 1) * 3072]), fence=True)
            for no, nl in nchunks(TC, 512):
                ps = ps_a if (no // 512) % 2 == 0 else ps_b
                mi = 0
                for k in range(3):
                    for kc in range(8):
                        p.t(lambda t, oc=oc, k=k, kc=kc, no=no, nl=nl, ps=ps, mi=mi:
                            t.matmul(ps[:, 0:nl],
                                     wstr[:, (oc % 2) * 3072 + (k * 8 + kc) * 128:
                                          (oc % 2) * 3072 + (k * 8 + kc) * 128 + 128],
                                     xn1_bf[:, kc * TW1 + no + k:kc * TW1 + no + k + nl],
                                     start=(k == 0 and kc == 0), stop=(k == 2 and kc == 7)))
                        mi += 1
                p.v(lambda v, oc=oc, no=no, nl=nl, ps=ps:
                    v.tensor_scalar(xc_bf[:, oc * TC + no:oc * TC + no + nl], ps[:, 0:nl],
                                    convb_s[:, oc:oc + 1], None, ALU.add))
        run_prog(nc, p, "d2")
        xst.close()

        p = Prog(nc)
        for oc in range(16):
            p.dma("gpsimd", lambda g, oc=oc:
                  g.dma_start(out=wstr[:, (oc % 2) * 3072:(oc % 2) * 3072 + 2048],
                              in_=in_wt[:, oc * 2048:(oc + 1) * 2048]), fence=True)
            for no, nl in nchunks(TC, 512):
                ps = ps_a if (no // 512) % 2 == 0 else ps_b
                for kc in range(16):
                    p.t(lambda t, oc=oc, kc=kc, no=no, nl=nl, ps=ps:
                        t.matmul(ps[:, 0:nl],
                                 wstr[:, (oc % 2) * 3072 + kc * 128:(oc % 2) * 3072 + kc * 128 + 128],
                                 xc_bf[:, kc * TC + no:kc * TC + no + nl],
                                 start=(kc == 0), stop=(kc == 15)))
                p.s(lambda s, oc=oc, ps=ps, nl=nl:
                    s.activation(u_s[:, 0:nl], ps[:, 0:nl], AF.Silu, bias=inb_s[:, oc:oc + 1]))
                p.dma("sync", lambda sy, oc=oc, no=no, nl=nl:
                      sy.dma_start(out=u_o[oc * 128:(oc + 1) * 128, no:no + nl],
                                   in_=u_s[:, 0:nl]))
        run_prog(nc, p, "d2b")

        p = Prog(nc)
        with contextlib.ExitStack() as d3s:
            bcw = d3s.enter_context(nc.sbuf_tensor("bcw", [128, 2 * 16 * 16], BF16))
            for kc in range(16):
                p.dma("gpsimd", lambda g, kc=kc:
                      g.dma_start(out=bcw[:, kc * 16:(kc + 1) * 16],
                                  in_=bp_t[kc * 128:(kc + 1) * 128, :]))
                p.dma("gpsimd", lambda g, kc=kc:
                      g.dma_start(out=bcw[:, 256 + kc * 16:256 + (kc + 1) * 16],
                                  in_=cp_t[kc * 128:(kc + 1) * 128, :]))
            for no, nl in nchunks(TC, 512):
                for kc in range(16):
                    p.dma("gpsimd", lambda g, kc=kc, no=no, nl=nl:
                          g.dma_start(out=u_sl[:, kc * 512:kc * 512 + nl],
                                      in_=u_o[kc * 128:(kc + 1) * 128, no:no + nl]))
                for oc in range(16):
                    p.dma("gpsimd", lambda g, oc=oc:
                          g.dma_start(out=wstr[:, (oc % 2) * 3072:(oc % 2) * 3072 + 2048],
                                      in_=dt_wt[:, oc * 2048:(oc + 1) * 2048]), fence=True)
                    ps = ps_a if oc % 2 == 0 else ps_b
                    for kc in range(16):
                        p.t(lambda t, oc=oc, kc=kc, nl=nl, ps=ps:
                            t.matmul(ps[:, 0:nl],
                                     wstr[:, (oc % 2) * 3072 + kc * 128:(oc % 2) * 3072 + kc * 128 + 128],
                                     u_sl[:, kc * 512:kc * 512 + nl],
                                     start=(kc == 0), stop=(kc == 15)))
                    p.v(lambda v, oc=oc, ps=ps, nl=nl:
                        v.tensor_scalar(t1[:, 0:nl], ps[:, 0:nl], dtb_s[:, oc:oc + 1],
                                        None, ALU.add))
                    p.s(lambda s, nl=nl: s.activation(t2[:, 0:nl], t1[:, 0:nl], AF.Abs))
                    p.s(lambda s, nl=nl: s.activation(t2[:, 0:nl], t2[:, 0:nl], AF.Exp, scale=-1.0))
                    p.s(lambda s, nl=nl: s.activation(t2[:, 0:nl], t2[:, 0:nl], AF.Ln, bias=1.0))
                    p.v(lambda v, nl=nl: v.tensor_scalar(t1[:, 0:nl], t1[:, 0:nl], 0.0,
                                                         None, ALU.max))
                    p.v(lambda v, nl=nl: v.tensor_add(t1[:, 0:nl], t1[:, 0:nl], t2[:, 0:nl]))
                    p.dma("sync", lambda sy, oc=oc, no=no, nl=nl:
                          sy.dma_start(out=dt_o[oc * 128:(oc + 1) * 128, no:no + nl],
                                       in_=t1[:, 0:nl]))
                for boff, r0, bias in ((0, 0, bpb_s), (256, 32, cpb_s)):
                    ps = ps_a
                    for kc in range(16):
                        p.t(lambda t, kc=kc, nl=nl, ps=ps, boff=boff:
                            t.matmul(ps[0:16, 0:nl], bcw[:, boff + kc * 16:boff + kc * 16 + 16],
                                     u_sl[:, kc * 512:kc * 512 + nl],
                                     start=(kc == 0), stop=(kc == 15)))
                    p.v(lambda v, no=no, nl=nl, ps=ps, bias=bias, r0=r0:
                        v.tensor_scalar(bc_s[r0:r0 + 16, no:no + nl], ps[0:16, 0:nl],
                                        bias[0:16, 0:1], None, ALU.add))
            p.dma("sync", lambda sy: sy.dma_start(out=b_o, in_=bc_s[0:16, :]))
            p.dma("sync", lambda sy: sy.dma_start(out=c_o, in_=bc_s[32:48, :]))
            run_prog(nc, p, "d3")
        ps_stack.close()
    return nc


def host_prep_D(inp, x1_full, core):
    b, h = core // 2, core % 2
    t0 = h * TC
    xT = np.asarray(x1_full[b]).T
    xw = np.zeros((E, TW1), np.float32)
    lo, hi = t0 - 1, t0 + TC + 1
    slo, shi = max(lo, 0), min(hi, S)
    xw[:, slo - lo:shi - lo] = xT[:, slo:shi]
    cw = np.asarray(inp["L1_attn_conv_w"])
    cwt = np.zeros((128, 24 * 16 * 128), np.float32)
    for oc in range(16):
        for k in range(3):
            for kc in range(8):
                tile = cw[oc * 128:(oc + 1) * 128, kc * 128:(kc + 1) * 128, k].T
                cwt[:, (oc * 24 + k * 8 + kc) * 128:(oc * 24 + k * 8 + kc) * 128 + 128] = tile
    def tile2048(w):
        wt = np.zeros((128, 16 * 16 * 128), np.float32)
        for oc in range(16):
            for kc in range(16):
                wt[:, (oc * 16 + kc) * 128:(oc * 16 + kc) * 128 + 128] = \
                    w[oc * 128:(oc + 1) * 128, kc * 128:(kc + 1) * 128].T
        return wt
    col = lambda a, n: np.ascontiguousarray(np.asarray(a).reshape(n, 128).T)
    return {
        "xw1": np.ascontiguousarray(xw),
        "conv_wt": np.ascontiguousarray(cwt),
        "convb_c": col(inp["L1_attn_conv_b"], 16),
        "in_wt": tile2048(np.asarray(inp["L1_attn_in_w"])),
        "inb_c": col(inp["L1_attn_in_b"], 16),
        "dt_wt": tile2048(np.asarray(inp["L1_attn_dt_w"])),
        "dtb_c": col(inp["L1_attn_dt_b"], 16),
        "bp_t": np.ascontiguousarray(np.asarray(inp["L1_attn_Bp_w"]).T),
        "cp_t": np.ascontiguousarray(np.asarray(inp["L1_attn_Cp_w"]).T),
        "bpb_c": np.ascontiguousarray(np.asarray(inp["L1_attn_Bp_b"]).reshape(16, 1)),
        "cpb_c": np.ascontiguousarray(np.asarray(inp["L1_attn_Cp_b"]).reshape(16, 1)),
        "ones_c": np.ones((128, 1), np.float32),
        "ones_r_d": np.ones((1, 128), np.float32),
    }





def build_launch_F():
    nc = bass.Bass("TRN2", target_bir_lowering=False)

    def din(name, shape, dt=F32):
        return nc.dram_tensor(name, shape, dt, kind="ExternalInput").ap()

    x1_ft = din("x1_ft", [E, TC])
    x1_tk = din("x1_tk", [TC, E])
    wg1 = din("wg1", [E, 8])
    bg1_row = din("bg1_row", [1, 8])
    w1 = din("w1", [8 * E, 4096])
    b1c = din("b1c", [128, 8 * 32])
    w2 = din("w2", [8 * 4096, E])
    b2r = din("b2r", [8, E])
    dskip_c = din("dskip_c", [128, 16])
    out_wt = din("out_wt", [DI, E])
    outb1_row = din("outb1_row", [1, E])
    y1ssm = din("y1ssm", [DI, TC])
    u1_own = din("u1_own", [DI, TC])
    ones_c = din("ones_c", [128, 1])
    ones_r_d = din("ones_r_d", [1, 128])

    out_o = nc.dram_tensor("out_o", [TC, E], F32, kind="ExternalOutput").ap()
    g_scr = nc.dram_tensor("g_scr", [TC, E], F32)

    with contextlib.ExitStack() as st0:
        sb = lambda n, shp, dt: st0.enter_context(nc.sbuf_tensor(n, shp, dt))
        ones_f = sb("ones_f", [128, 1], F32)
        ones_r = sb("ones_r", [1, 128], F32)
        ones_rb = sb("ones_rb", [1, 128], BF16)
        dskip_s = sb("dskip_s", [128, 16], F32)
        b1_s = sb("b1_s", [128, 8 * 32], F32)
        b2_b = sb("b2_b", [1, 8 * E], BF16)
        strow = sb("strow", [1, 6 * 512], F32)
        xsq = sb("xsq", [128, 512], F32)
        ind_all = sb("ind_all", [128, 128], F32)
        gate_all = sb("gate_all", [128, 128], F32)
        stat16 = sb("stat16", [128, 16], F32)
        xn1_bf = sb("xn1_bf", [128, 8 * TC], BF16)

        p = Prog(nc)
        p.dma("sync", lambda sy: sy.dma_start(out=ones_f[:], in_=ones_c))
        p.dma("sync", lambda sy: sy.dma_start(out=ones_r[:], in_=ones_r_d))
        p.dma("gpsimd", lambda g: g.dma_start(out=ones_rb[:], in_=ones_r_d))
        p.dma("sync", lambda sy: sy.dma_start(out=dskip_s[:], in_=dskip_c))
        p.dma("sync", lambda sy: sy.dma_start(out=b1_s[:], in_=b1c))
        for e in range(8):
            p.dma("gpsimd", lambda g, e=e:
                  g.dma_start(out=b2_b[0:1, e * E:(e + 1) * E], in_=b2r[e:e + 1, :]))

        with contextlib.ExitStack() as f1s:
            xw8 = f1s.enter_context(nc.sbuf_tensor("xw8", [128, 8 * 512], F32))
            wgb = f1s.enter_context(nc.sbuf_tensor("wgb", [128, 8 * 8], BF16))
            bgb = f1s.enter_context(nc.sbuf_tensor("bgb", [1, 8], BF16))
            ps_a = f1s.enter_context(nc.psum_tensor("ps_a", [128, 512], F32))
            ps_st = f1s.enter_context(nc.psum_tensor("ps_st", [1, 1024], F32))
            ps_r = f1s.enter_context(nc.psum_tensor("ps_r", [128, 512], F32))
            ps_m = f1s.enter_context(nc.psum_tensor("ps_m", [128, 512], F32))
            for kc in range(8):
                p.dma("gpsimd", lambda g, kc=kc:
                      g.dma_start(out=wgb[:, kc * 8:(kc + 1) * 8],
                                  in_=wg1[kc * 128:(kc + 1) * 128, :]))
            p.dma("gpsimd", lambda g: g.dma_start(out=bgb[:], in_=bg1_row))
            for c0c, wl in nchunks(TC, 512):
                for kc in range(8):
                    p.dma("sync", lambda sy, kc=kc, c0c=c0c, wl=wl:
                          sy.dma_start(out=xw8[:, kc * 512:kc * 512 + wl],
                                       in_=x1_ft[kc * 128:(kc + 1) * 128, c0c:c0c + wl]))
                for kc in range(8):
                    p.t(lambda t, kc=kc, wl=wl: t.matmul(ps_st[0:1, 0:wl], ones_f[:, 0:1],
                                                         xw8[:, kc * 512:kc * 512 + wl],
                                                         start=(kc == 0), stop=(kc == 7)))
                for kc in range(8):
                    p.s(lambda s, kc=kc, wl=wl: s.activation(xsq[:, 0:wl],
                                                             xw8[:, kc * 512:kc * 512 + wl],
                                                             AF.Square))
                    p.t(lambda t, kc=kc, wl=wl: t.matmul(ps_st[0:1, 512:512 + wl], ones_f[:, 0:1],
                                                         xsq[:, 0:wl], start=(kc == 0),
                                                         stop=(kc == 7)))
                p.v(lambda v, wl=wl: v.tensor_scalar(strow[0:1, 0:wl], ps_st[0:1, 0:wl],
                                                     1.0 / 1024.0, None, ALU.mult))
                p.v(lambda v, wl=wl: v.tensor_scalar(strow[0:1, 512:512 + wl],
                                                     ps_st[0:1, 512:512 + wl],
                                                     1.0 / 1024.0, None, ALU.mult))
                p.v(lambda v, wl=wl: v.tensor_mul(strow[0:1, 1024:1024 + wl], strow[0:1, 0:wl],
                                                  strow[0:1, 0:wl]))
                p.v(lambda v, wl=wl: v.tensor_sub(strow[0:1, 1024:1024 + wl],
                                                  strow[0:1, 512:512 + wl],
                                                  strow[0:1, 1024:1024 + wl]))
                p.v(lambda v, wl=wl: v.tensor_scalar(strow[0:1, 1024:1024 + wl],
                                                     strow[0:1, 1024:1024 + wl],
                                                     1e-5, None, ALU.add))
                p.s(lambda s, wl=wl: s.activation(strow[0:1, 1536:1536 + wl],
                                                  strow[0:1, 1024:1024 + wl], AF.Ln))
                p.s(lambda s, wl=wl: s.activation(strow[0:1, 1536:1536 + wl],
                                                  strow[0:1, 1536:1536 + wl], AF.Exp, scale=-0.5))
                p.v(lambda v, wl=wl: v.tensor_mul(strow[0:1, 2048:2048 + wl], strow[0:1, 0:wl],
                                                  strow[0:1, 1536:1536 + wl]))
                p.v(lambda v, wl=wl: v.tensor_scalar(strow[0:1, 2048:2048 + wl],
                                                     strow[0:1, 2048:2048 + wl],
                                                     -1.0, None, ALU.mult))
                p.t(lambda t, wl=wl: t.matmul(ps_r[:, 0:wl], ones_r[0:1, :],
                                              strow[0:1, 1536:1536 + wl], start=True, stop=True))
                p.t(lambda t, wl=wl: t.matmul(ps_m[:, 0:wl], ones_r[0:1, :],
                                              strow[0:1, 2048:2048 + wl], start=True, stop=True))
                for kc in range(8):
                    p.v(lambda v, kc=kc, wl=wl: v.tensor_mul(xw8[:, kc * 512:kc * 512 + wl],
                                                             xw8[:, kc * 512:kc * 512 + wl],
                                                             ps_r[:, 0:wl]))
                    p.v(lambda v, kc=kc, c0c=c0c, wl=wl:
                        v.tensor_tensor(xn1_bf[:, kc * TC + c0c:kc * TC + c0c + wl],
                                        xw8[:, kc * 512:kc * 512 + wl],
                                        ps_m[:, 0:wl], ALU.add))
            for tt in range(16):
                tt0 = tt * 128
                for kc in range(8):
                    p.t(lambda t, kc=kc, tt0=tt0:
                        t.matmul(ps_a[:, 0:8], xn1_bf[:, kc * TC + tt0:kc * TC + tt0 + 128],
                                 wgb[:, kc * 8:(kc + 1) * 8], start=(kc == 0), stop=False))
                p.t(lambda t: t.matmul(ps_a[:, 0:8], ones_rb[0:1, 0:128],
                                       bgb[0:1, :], start=False, stop=True))
                p.v(lambda v, tt=tt: v.tensor_copy(gate_all[:, tt * 8:(tt + 1) * 8],
                                                   ps_a[:, 0:8]))
            g3 = gate_all[:, :].rearrange("p (t e) -> p t e", t=16, e=8)
            p.v(lambda v: v.reduce_max(stat16[:, 0:16], g3, axis=AX))
            p.v(lambda v: v.tensor_scalar(gate_all[:, :], gate_all[:, :], 1.0, None, ALU.mult))
            for tt in range(16):
                p.v(lambda v, tt=tt: v.tensor_scalar(ind_all[:, tt * 8:(tt + 1) * 8],
                                                     gate_all[:, tt * 8:(tt + 1) * 8],
                                                     stat16[:, tt:tt + 1], None, ALU.is_equal))
            run_prog(nc, p, "f1")

        p = Prog(nc)
        with contextlib.ExitStack() as f2s:
            sb2 = lambda n, shp, dt: f2s.enter_context(nc.sbuf_tensor(n, shp, dt))
            h_bf = sb2("h_bf", [128, 32 * 1024], BF16)
            wstr = sb2("wstr", [128, 2 * 1024], BF16)
            w2str = sb2("w2str", [128, 2 * 1024], BF16)
            moe_h = sb2("moe_h", [128, 8 * 1024], F32)
            gh = sb2("gh", [128, 1024], F32)
            ps_y = [f2s.enter_context(nc.psum_tensor(f"ps_y{i}", [128, 512], F32))
                    for i in range(8)]
            ps_h0 = ps_y[0]
            for th in range(2):
                tb0 = th * 1024
                for e in range(8):
                    for jc in range(32):
                        s0w = w1[0:1, 0:1]
                        sap = bass.AP(w1.tensor, (e * 1024) * 4096 + jc * 128,
                                      [[4096, 128], [128 * 4096, 8], [1, 128]])
                        p.dma("gpsimd", lambda g, sap=sap, jc=jc:
                              g.dma_start(out=wstr[:, (jc % 2) * 1024:(jc % 2) * 1024 + 1024],
                                          in_=sap), fence=True)
                        for no, nl in nchunks(1024, 512):
                            psh = ps_h0
                            for dk in range(8):
                                p.t(lambda t, jc=jc, dk=dk, no=no, nl=nl, tb0=tb0, psh=psh:
                                    t.matmul(psh[:, 0:nl],
                                             wstr[:, (jc % 2) * 1024 + dk * 128:(jc % 2) * 1024 + dk * 128 + 128],
                                             xn1_bf[:, dk * TC + tb0 + no:dk * TC + tb0 + no + nl],
                                             start=(dk == 0), stop=(dk == 7)))
                            p.s(lambda s, jc=jc, no=no, nl=nl, e=e, psh=psh:
                                s.activation(h_bf[:, jc * 1024 + no:jc * 1024 + no + nl],
                                             psh[:, 0:nl], AF.Relu,
                                             bias=b1_s[:, e * 32 + jc:e * 32 + jc + 1]))
                    for dh in range(2):
                        for jc in range(32):
                            p.dma("gpsimd", lambda g, e=e, jc=jc, dh=dh:
                                  g.dma_start(out=w2str[:, (jc % 2) * 1024:(jc % 2) * 1024 + 512],
                                              in_=w2[e * 4096 + jc * 128:e * 4096 + (jc + 1) * 128,
                                                     dh * 512:(dh + 1) * 512]), fence=True)
                            for tl in range(8):
                                p.t(lambda t, jc=jc, tl=tl, dh=dh, psy=ps_y[tl]:
                                    t.matmul(psy[:, 0:512],
                                             h_bf[:, jc * 1024 + tl * 128:jc * 1024 + tl * 128 + 128],
                                             w2str[:, (jc % 2) * 1024:(jc % 2) * 1024 + 512],
                                             start=(jc == 0), stop=False))
                        for tl in range(8):
                            p.t(lambda t, tl=tl, e=e, dh=dh, psy=ps_y[tl]:
                                t.matmul(psy[:, 0:512], ones_rb[0:1, 0:128],
                                         b2_b[0:1, e * E + dh * 512:e * E + dh * 512 + 512],
                                         start=False, stop=True))
                        for tl in range(8):
                            tt = th * 8 + tl
                            if e == 0:
                                p.v(lambda v, tl=tl, dh=dh, tt=tt, psy=ps_y[tl]:
                                    v.tensor_scalar(moe_h[:, tl * 1024 + dh * 512:tl * 1024 + dh * 512 + 512],
                                                    psy[:, 0:512],
                                                    ind_all[:, tt * 8:tt * 8 + 1], None, ALU.mult))
                            else:
                                p.v(lambda v, tl=tl, dh=dh, tt=tt, e=e, psy=ps_y[tl]:
                                    v.scalar_tensor_tensor(
                                        moe_h[:, tl * 1024 + dh * 512:tl * 1024 + dh * 512 + 512],
                                        psy[:, 0:512], ind_all[:, tt * 8 + e:tt * 8 + e + 1],
                                        moe_h[:, tl * 1024 + dh * 512:tl * 1024 + dh * 512 + 512],
                                        ALU.mult, ALU.add))
                for tl in range(8):
                    tt = th * 8 + tl
                    p.s(lambda s, tl=tl: s.activation(gh[:, :], moe_h[:, tl * 1024:(tl + 1) * 1024],
                                                      AF.Silu))
                    p.dma("sync", lambda sy, tt=tt:
                          sy.dma_start(out=g_scr.ap()[tt * 128:(tt + 1) * 128, :], in_=gh[:, :]))
            run_prog(nc, p, "f2")

        p = Prog(nc)
        with contextlib.ExitStack() as f3s:
            sb3 = lambda n, shp, dt: f3s.enter_context(nc.sbuf_tensor(n, shp, dt))
            ssm_bf = sb3("ssm_bf", [128, 16 * TC], BF16)
            w_out = sb3("w_out", [128, 16 * 1024], BF16)
            outb_b = sb3("outb_b", [1, E], BF16)
            ych = sb3("ych", [128, TC], F32)
            uch = sb3("uch", [128, TC], F32)
            g6 = sb3("g6", [128, 1024], F32)
            xk6 = sb3("xk6", [128, 1024], F32)
            x16 = sb3("x16", [128, 1024], F32)
            ps_o0 = f3s.enter_context(nc.psum_tensor("ps_o0", [128, 1024], F32))
            ps_o1 = f3s.enter_context(nc.psum_tensor("ps_o1", [128, 1024], F32))
            for kc in range(16):
                p.dma("gpsimd", lambda g, kc=kc:
                      g.dma_start(out=w_out[:, kc * 1024:(kc + 1) * 1024],
                                  in_=out_wt[kc * 128:(kc + 1) * 128, :]))
            p.dma("gpsimd", lambda g: g.dma_start(out=outb_b[:], in_=outb1_row))
            for fc in range(16):
                p.dma("sync", lambda sy, fc=fc:
                      sy.dma_start(out=ych[:, :], in_=y1ssm[fc * 128:(fc + 1) * 128, :]))
                p.dma("sync", lambda sy, fc=fc:
                      sy.dma_start(out=uch[:, :], in_=u1_own[fc * 128:(fc + 1) * 128, :]))
                p.v(lambda v, fc=fc: v.scalar_tensor_tensor(ssm_bf[:, fc * TC:(fc + 1) * TC],
                                                            uch[:, :], dskip_s[:, fc:fc + 1],
                                                            ych[:, :], ALU.mult, ALU.add))
            for tt in range(16):
                tt0 = tt * 128
                pso = ps_o0 if tt % 2 == 0 else ps_o1
                for half in range(2):
                    for fc in range(16):
                        p.t(lambda t, fc=fc, tt0=tt0, half=half, pso=pso:
                            t.matmul(pso[:, half * 512:(half + 1) * 512],
                                     ssm_bf[:, fc * TC + tt0:fc * TC + tt0 + 128],
                                     w_out[:, fc * 1024 + half * 512:fc * 1024 + half * 512 + 512],
                                     start=(fc == 0), stop=False))
                    p.t(lambda t, half=half, pso=pso:
                        t.matmul(pso[:, half * 512:(half + 1) * 512], ones_rb[0:1, 0:128],
                                 outb_b[0:1, half * 512:(half + 1) * 512],
                                 start=False, stop=True))
                p.dma("sync", lambda sy, tt=tt:
                      sy.dma_start(out=g6[:, :], in_=g_scr.ap()[tt * 128:(tt + 1) * 128, :]))
                p.dma("sync", lambda sy, tt=tt:
                      sy.dma_start(out=xk6[:, :], in_=x1_tk[tt * 128:(tt + 1) * 128, :]))
                for half in range(2):
                    p.v(lambda v, half=half, pso=pso:
                        v.tensor_mul(x16[:, half * 512:(half + 1) * 512],
                                     pso[:, half * 512:(half + 1) * 512],
                                     g6[:, half * 512:(half + 1) * 512]))
                p.v(lambda v: v.tensor_add(x16[:, :], x16[:, :], xk6[:, :]))
                p.dma("sync", lambda sy, tt=tt:
                      sy.dma_start(out=out_o[tt * 128:(tt + 1) * 128, :], in_=x16[:, :]))
            run_prog(nc, p, "f3")
    return nc


def host_prep_F(inp, x1_full, y1ssm_full, u1_cores, core):
    b, h = core // 2, core % 2
    t0 = h * TC
    x1b = np.asarray(x1_full[b])
    col = lambda a, n: np.ascontiguousarray(np.asarray(a).reshape(n, 128).T)
    b1 = np.asarray(inp["L1_ffn_b1"])
    b1c = np.zeros((128, 8 * 32), np.float32)
    for e in range(8):
        b1c[:, e * 32:(e + 1) * 32] = b1[e].reshape(32, 128).T
    return {
        "x1_ft": np.ascontiguousarray(x1b[t0:t0 + TC, :].T),
        "x1_tk": np.ascontiguousarray(x1b[t0:t0 + TC, :]),
        "wg1": np.ascontiguousarray(np.asarray(inp["L1_ffn_Wg"])),
        "bg1_row": np.ascontiguousarray(np.asarray(inp["L1_ffn_bg"]).reshape(1, 8)),
        "w1": np.ascontiguousarray(np.asarray(inp["L1_ffn_W1"]).reshape(8 * E, 4096)),
        "b1c": b1c,
        "w2": np.ascontiguousarray(np.asarray(inp["L1_ffn_W2"]).reshape(8 * 4096, E)),
        "b2r": np.ascontiguousarray(np.asarray(inp["L1_ffn_b2"])),
        "dskip_c": col(inp["L1_attn_Dskip"], 16),
        "out_wt": np.ascontiguousarray(np.asarray(inp["L1_attn_out_w"]).T),
        "outb1_row": np.ascontiguousarray(np.asarray(inp["L1_attn_out_b"]).reshape(1, E)),
        "y1ssm": np.ascontiguousarray(y1ssm_full[b, t0:t0 + TC, :].T),
        "u1_own": np.ascontiguousarray(np.asarray(u1_cores[core])),
        "ones_c": np.ones((128, 1), np.float32),
        "ones_r_d": np.ones((1, 128), np.float32),
    }


def kernel(**inputs):
    inp = inputs
    cores = list(range(NCORES))

    ncA = build_launch_A()
    resA = run_bass_kernel_spmd(ncA, [host_prep_A(inp, c) for c in cores], core_ids=cores)
    rA = resA.results

    dt0 = np.stack([np.concatenate([rA[2 * b]["dt_o"].T, rA[2 * b + 1]["dt_o"].T], axis=0)
                    for b in range(B)])
    u0 = np.stack([np.concatenate([rA[2 * b]["u_o"].T, rA[2 * b + 1]["u_o"].T], axis=0)
                   for b in range(B)])
    Bm0 = np.stack([np.concatenate([rA[2 * b]["b_o"].T, rA[2 * b + 1]["b_o"].T], axis=0)
                    for b in range(B)])
    Cm0 = np.stack([np.concatenate([rA[2 * b]["c_o"].T, rA[2 * b + 1]["c_o"].T], axis=0)
                    for b in range(B)])
    ncS = build_launch_scan()
    resB = run_bass_kernel_spmd(
        ncS, [host_prep_scan(dt0, u0, Bm0, Cm0, inp["L0_attn_A_log"], c) for c in cores],
        core_ids=cores)
    yssm0 = np.stack([np.concatenate([resB.results[2 * b]["y_o"],
                                      resB.results[2 * b + 1]["y_o"]], axis=0).T
                      for b in range(B)])

    u_cores = [rA[c]["u_o"] for c in cores]
    ncC = build_launch_C()
    resC = run_bass_kernel_spmd(
        ncC, [host_prep_C(inp, None, yssm0, u_cores, c) for c in cores], core_ids=cores)
    x1 = np.stack([np.concatenate([resC.results[2 * b]["x1_o"],
                                   resC.results[2 * b + 1]["x1_o"]], axis=0)
                   for b in range(B)])

    ncD = build_launch_D()
    resD = run_bass_kernel_spmd(ncD, [host_prep_D(inp, x1, c) for c in cores],
                                core_ids=cores)
    rD = resD.results

    dt1 = np.stack([np.concatenate([rD[2 * b]["dt_o"].T, rD[2 * b + 1]["dt_o"].T], axis=0)
                    for b in range(B)])
    u1 = np.stack([np.concatenate([rD[2 * b]["u_o"].T, rD[2 * b + 1]["u_o"].T], axis=0)
                   for b in range(B)])
    B1 = np.stack([np.concatenate([rD[2 * b]["b_o"].T, rD[2 * b + 1]["b_o"].T], axis=0)
                   for b in range(B)])
    C1 = np.stack([np.concatenate([rD[2 * b]["c_o"].T, rD[2 * b + 1]["c_o"].T], axis=0)
                   for b in range(B)])
    resE = run_bass_kernel_spmd(
        ncS, [host_prep_scan(dt1, u1, B1, C1, inp["L1_attn_A_log"], c) for c in cores],
        core_ids=cores)
    yssm1 = np.stack([np.concatenate([resE.results[2 * b]["y_o"],
                                      resE.results[2 * b + 1]["y_o"]], axis=0).T
                      for b in range(B)])

    u1_cores = [rD[c]["u_o"] for c in cores]
    ncF = build_launch_F()
    resF = run_bass_kernel_spmd(
        ncF, [host_prep_F(inp, x1, yssm1, u1_cores, c) for c in cores], core_ids=cores)
    out = np.stack([np.concatenate([resF.results[2 * b]["out_o"],
                                    resF.results[2 * b + 1]["out_o"]], axis=0)
                    for b in range(B)])
    return out.astype(np.float32)
